# version 22
# baseline (speedup 1.0000x reference)
import numpy as np
from contextlib import ExitStack
import concourse.bass as bass
import concourse.mybir as mybir
from concourse.bass_utils import run_bass_kernel_spmd

F32 = mybir.dt.float32
BF16 = mybir.dt.bfloat16
AF = mybir.ActivationFunctionType
ALU = mybir.AluOpType

D, NH, DH, NCD, DHID, NCH = 1024, 16, 64, 8, 4096, 32
NT = 512
EPS_LN = 1e-5
EPS_ATT = 1e-6
ATT_SCALE = DH ** -0.5

ENGS = ("pe", "act", "dve", "sy", "gp")


class Builder:
    def __init__(self, S, stage="full"):
        self.S = S
        self.T = S // 4
        self.TSUP = min(2048, S)
        self.QM = min(1024, self.T)
        self.TH = min(1024, self.T)
        self.stage = stage
        self.nc = bass.Bass("TRN2", target_bir_lowering=False)
        self.ctx = ExitStack()
        self.streams = {e: [] for e in ENGS}
        self.sems = {}
        self.cnt = {}
        self.bar_n = 0
        self._uid = 0

    def sem(self, name):
        h = self.ctx.enter_context(self.nc.semaphore(name))
        self.sems[name] = h
        self.cnt[name] = 0
        return name

    def emit(self, eng, fn, waits=(), incs=()):
        for s, n in incs:
            self.cnt[s] += n
        wl = [(self.sems[s], v) for s, v in waits]
        il = [(self.sems[s], n) for s, n in incs]

        def closure(e, fn=fn, wl=wl, il=il):
            for h, v in wl:
                e.wait_ge(h, v)
            inst = fn(e) if fn is not None else None
            for h, n in il:
                assert inst is not None
                inst.then_inc(h, n)
        self.streams[eng].append(closure)

    def c(self, s):
        return self.cnt[s]

    def u(self, base):
        self._uid += 1
        return f"{base}_{self._uid}"

    def warmup(self, n=24, extra_waits=()):
        with self.nc.psum_tensor(self.u("warm"), [128, NT], F32) as wp:
            def mk(e, wp=wp, n=n):
                for _ in range(n):
                    inst = e.matmul(wp[:], self.warmsrc[:, 0:128],
                                    self.warmsrc[:, 128:640], start=True, stop=True)
                return inst
            self.emit("pe", mk, waits=[("gp", 1)] + list(extra_waits) + self.guard(),
                      incs=[("pe", 1)])

    def guard(self):
        g = []
        if self.cnt["dve"]:
            g.append(("dve", self.cnt["dve"]))
        if self.cnt["act"]:
            g.append(("act", self.cnt["act"]))
        return g

    def barrier(self):
        self.bar_n += 1
        k = self.bar_n
        waits = [(s, self.cnt[s]) for s in self.cnt if s != "bar" and self.cnt[s] > 0]
        self.emit("sy", lambda e: e.sem_inc(self.sems["bar"], 1), waits=waits)
        self.cnt["bar"] += 1
        for eng in ("pe", "act", "dve", "gp"):
            self.emit(eng, None, waits=[("bar", k)])

    def build(self):
        nc, S, T = self.nc, self.S, self.T
        ctx = self.ctx

        self.xT = nc.dram_tensor("xT", [D, S], F32, kind="ExternalInput").ap()
        self.wq = nc.dram_tensor("wq", [128, NCD * D], F32, kind="ExternalInput").ap()
        self.wk = nc.dram_tensor("wk", [128, NCD * D], F32, kind="ExternalInput").ap()
        self.wv = nc.dram_tensor("wv", [128, NCD * D], F32, kind="ExternalInput").ap()
        self.wo = nc.dram_tensor("wo", [128, NCD * D], F32, kind="ExternalInput").ap()
        self.w1 = nc.dram_tensor("w1", [NCH, 128, NCD * 128], F32, kind="ExternalInput").ap()
        self.w2 = nc.dram_tensor("w2", [NCH, 128, NCD * 128], F32, kind="ExternalInput").ap()
        self.h1 = nc.dram_tensor("h1", [NCH, 128, NCH * 128], F32, kind="ExternalInput").ap()
        self.h2 = nc.dram_tensor("h2", [NCH, 128, NCH * 128], F32, kind="ExternalInput").ap()
        self.w3 = nc.dram_tensor("w3", [NCD, 128, NCH * 128], F32, kind="ExternalInput").ap()
        self.s1 = nc.dram_tensor("s1", [128, NCH], F32, kind="ExternalInput").ap()
        self.s2 = nc.dram_tensor("s2", [128, NCH], F32, kind="ExternalInput").ap()
        self.hs1 = nc.dram_tensor("hs1", [128, NCH], F32, kind="ExternalInput").ap()
        self.hs2 = nc.dram_tensor("hs2", [128, NCH], F32, kind="ExternalInput").ap()
        self.s3 = nc.dram_tensor("s3", [128, NCD], F32, kind="ExternalInput").ap()
        self.Esel = nc.dram_tensor("Esel", [16, D], F32, kind="ExternalInput").ap()
        self.out = nc.dram_tensor("out", [D, T], F32, kind="ExternalOutput").ap()

        self.K_sc = nc.dram_tensor("K_sc", [NCD, 128, S], BF16).ap()
        self.V_sc = nc.dram_tensor("V_sc", [NH, 128, S // 128, DH + 1], BF16).ap()
        self.y1_sc = nc.dram_tensor("y1_sc", [NCD, 128, T], F32).ap()
        self.RS_sc = nc.dram_tensor("RS_sc", [16, T], F32).ap()

        for s in ["bar", "pe", "act", "dve", "gp",
                  "ld_x0", "ld_x1",
                  "ld_w0", "ld_w1", "ld_w2",
                  "ld_k0", "ld_k1",
                  "ld_k0p0", "ld_k0p1", "ld_k0p2", "ld_k0p3",
                  "ld_k1p0", "ld_k1p1", "ld_k1p2", "ld_k1p3",
                  "ld_v0", "ld_v1", "ld_v2", "ld_v3",
                  "ld_y0", "ld_y1",
                  "ld_c",
                  "st_k0", "st_k1",
                  "st_v0", "st_v1",
                  "st_y0", "st_y1",
                  "st_rs", "st_r0", "st_r1",
                  "st_o0", "st_o1"]:
            self.sem(s)

        self.onesF = ctx.enter_context(nc.sbuf_tensor("onesF", [128, 1], F32))
        self.onesB = ctx.enter_context(nc.sbuf_tensor("onesB", [128, 1], BF16))
        self.onesR = ctx.enter_context(nc.sbuf_tensor("onesR", [1, 128], F32))
        self.s1b = ctx.enter_context(nc.sbuf_tensor("s1b", [128, NCH], F32))
        self.s2b = ctx.enter_context(nc.sbuf_tensor("s2b", [128, NCH], F32))
        self.hs1b = ctx.enter_context(nc.sbuf_tensor("hs1b", [128, NCH], F32))
        self.hs2b = ctx.enter_context(nc.sbuf_tensor("hs2b", [128, NCH], F32))
        self.s3b = ctx.enter_context(nc.sbuf_tensor("s3b", [128, NCD], F32))

        self.warmsrc = ctx.enter_context(nc.sbuf_tensor("warmsrc", [128, 640], BF16))
        self.emit("gp", lambda e: e.memset(self.warmsrc[:], 1.0), incs=[("gp", 1)])
        self.emit("gp", lambda e: e.memset(self.onesF[:], 1.0), incs=[("gp", 1)])
        self.emit("gp", lambda e: e.memset(self.onesB[:], 1.0), incs=[("gp", 1)])
        self.emit("gp", lambda e: e.memset(self.onesR[:], 1.0), incs=[("gp", 1)])
        for src, dst in [(self.s1, self.s1b), (self.s2, self.s2b),
                         (self.hs1, self.hs1b), (self.hs2, self.hs2b), (self.s3, self.s3b)]:
            self.emit("sy", lambda e, s=src, d=dst: e.dma_start(d[:], s), incs=[("ld_c", 16)])

        scope_q = ExitStack()
        scope_o = ExitStack()
        self.qbuf = scope_q.enter_context(nc.sbuf_tensor("qbuf", [128, NCD * T], BF16))
        self.phase0()
        self.barrier()
        if self.stage == "p0":
            self.debug_p0()
            self.finish()
            scope_q.close()
            scope_o.close()
            return
        self.obuf = scope_o.enter_context(nc.sbuf_tensor("obuf", [128, NCD * T], BF16, side="right"))
        scope_p1 = ExitStack()
        self.open_p1(scope_p1)
        TW = 512 if S <= 4096 else min(1024, T)
        qms_a = list(range(TW // 512))
        qms_b = list(range(TW // 512, T // 512))
        def p1_psums(pctx, kct, nkb=2):
            psS = [pctx.enter_context(nc.psum_tensor(self.u(f"psS{i}"), [128, kct * 512], F32))
                   for i in range(2)]
            psO = [pctx.enter_context(nc.psum_tensor(self.u(f"psO{i}"), [DH + 1, 512], F32))
                   for i in range(2)]
            self.wtil = [pctx.enter_context(nc.sbuf_tensor(self.u(f"wtil{i}"), [128, kct * 512], BF16))
                         for i in range(4)]
            self.wtil_free = [self.guard() for _ in range(4)]
            self.kbuf = [pctx.enter_context(nc.sbuf_tensor(self.u(f"kbuf{i}"), [128, S], BF16))
                         for i in range(nkb)]
            self.kbuf_free = [self.guard() for _ in range(nkb)]
            return psS, psO

        if self.stage == "p1":
            with ExitStack() as pctx:
                psS, psO = p1_psums(pctx, 3)
                for _ in self.gen_phase1(qms_a + qms_b, 3, psS, psO):
                    pass
            self.barrier()
            self.debug_p1()
            self.finish()
            scope_p1.close()
            scope_q.close()
            scope_o.close()
            return
        with ExitStack() as pctx:
            psS, psO = p1_psums(pctx, 3)
            for _ in self.gen_phase1(qms_a, 3, psS, psO):
                pass
        self.barrier()
        scope_lnA = ExitStack()
        ln2tA = scope_lnA.enter_context(nc.sbuf_tensor("ln2tA", [128, NCD * TW], BF16))
        self.phase2(0, TW, ln2tA)
        self.barrier()
        if self.stage == "p2":
            self.debug_p2()
            self.finish()
            scope_lnA.close()
            scope_p1.close()
            scope_q.close()
            scope_o.close()
            return
        def gen_ffn_a():
            for t0f in range(0, TW, 512):
                for _ in self.gen_phase3(t0f, 512, ln2tA, 0, TW):
                    yield
        if qms_b:
            with ExitStack() as pctx:
                psS, psO = p1_psums(pctx, 2, nkb=1)
                g_mid = self.gen_phase1(qms_b, 2, psS, psO)
                g_ffn = gen_ffn_a()
                mid_alive = ffn_alive = True
                while mid_alive or ffn_alive:
                    if mid_alive:
                        try:
                            next(g_mid)
                        except StopIteration:
                            mid_alive = False
                    if ffn_alive:
                        try:
                            next(g_ffn)
                        except StopIteration:
                            ffn_alive = False
        else:
            for _ in self.gen_phase3(0, 512 if T >= 512 else T, ln2tA, 0, TW,
                                     inline_stats=True):
                pass
        self.barrier()
        scope_lnA.close()
        scope_p1.close()
        scope_q.close()
        if qms_b:
            scope_lnB = ExitStack()
            ln2tB = scope_lnB.enter_context(nc.sbuf_tensor("ln2tB", [128, NCD * TW], BF16))
            self.phase2(TW, TW, ln2tB)
            self.barrier()
            scope_o.close()
            for t0f in range(TW, T, 1024):
                for _ in self.gen_phase3(t0f, min(1024, T - t0f), ln2tB, TW, TW,
                                         inline_stats=True):
                    pass
            scope_lnB.close()
        else:
            scope_o.close()
        self.finish()

    def phase0(self):
        nc, S, T, TSUP = self.nc, self.S, self.T, self.TSUP
        NSUP = S // TSUP
        NTT = TSUP // NT

        with ExitStack() as sc:
            wqb = sc.enter_context(nc.sbuf_tensor("wqb", [128, NCD * D], BF16))
            wkb = sc.enter_context(nc.sbuf_tensor("wkb", [128, NCD * D], BF16))
            wvb = sc.enter_context(nc.sbuf_tensor("wvb", [128, NCD * D], BF16))
            xs = [sc.enter_context(nc.sbuf_tensor(f"xs{i}", [128, TSUP], F32)) for i in range(2)]
            sq = [sc.enter_context(nc.sbuf_tensor(f"sq{i}", [128, TSUP], BF16)) for i in range(2)]
            xh = [sc.enter_context(nc.sbuf_tensor(f"xh{i}", [128, TSUP], BF16)) for i in range(2)]
            lnb = sc.enter_context(nc.sbuf_tensor("lnb", [128, NCD * TSUP], BF16))
            kvo = [sc.enter_context(nc.sbuf_tensor(f"kvo{i}", [128, TSUP], BF16)) for i in range(2)]
            vvo = [sc.enter_context(nc.sbuf_tensor(f"vvo{i}", [128, NH * (DH + 1)], BF16))
                   for i in range(2)]
            self.alloc_rows(sc, TSUP, "p0")
            for b in range(2):
                for hh in range(NH):
                    self.emit("gp", lambda e, b=b, hh=hh: e.memset(
                        vvo[b][:, hh * (DH + 1) + DH: hh * (DH + 1) + DH + 1], 1.0),
                        incs=[("gp", 1)])

            self.warmup()
            self.emit("gp", lambda e: e.dma_start(wqb[:], self.wq), incs=[("ld_w0", 16)])
            self.emit("gp", lambda e: e.dma_start(wkb[:], self.wk), incs=[("ld_w0", 16)])
            self.emit("gp", lambda e: e.dma_start(wvb[:], self.wv), incs=[("ld_w0", 16)])
            wproj_ready = self.c("ld_w0")

            xs_free = [[], []]
            sq_free = [[], []]
            kvo_free = [[], []]
            vvo_free = [[], []]

            for sup in range(NSUP):
                ts0 = sup * TSUP
                with ExitStack() as pctx:
                    ps_sum = pctx.enter_context(nc.psum_tensor(self.u("ps_sum"), [1, TSUP], F32))
                    ps_ssq = pctx.enter_context(nc.psum_tensor(self.u("ps_ssq"), [1, TSUP], F32))
                    for ic in range(NCD):
                        par = ic % 2
                        self.emit("sy",
                                  lambda e, p=par, i=ic, ts0=ts0: e.dma_start(
                                      xs[p][:], self.xT[i * 128:(i + 1) * 128, ts0:ts0 + TSUP]),
                                  waits=xs_free[par], incs=[(f"ld_x{par}", 16)])
                        ldv = self.c(f"ld_x{par}")
                        self.emit("act",
                                  lambda e, p=par: e.activation(sq[p][:], xs[p][:], AF.Square),
                                  waits=[(f"ld_x{par}", ldv)] + sq_free[par], incs=[("act", 1)])
                        self.emit("act",
                                  lambda e, p=par: e.activation(xh[p][:], xs[p][:], AF.Copy),
                                  waits=[(f"ld_x{par}", ldv)] + sq_free[par], incs=[("act", 1)])
                        actv = self.c("act")

                        def mk_stats(e, p=par, i=ic, ps_sum=ps_sum, ps_ssq=ps_ssq):
                            for tt in range(NTT):
                                e.matmul(ps_sum[0:1, tt * NT:(tt + 1) * NT], self.onesB[:],
                                         xh[p][:, tt * NT:(tt + 1) * NT],
                                         start=(i == 0), stop=(i == NCD - 1))
                            for tt in range(NTT):
                                inst = e.matmul(ps_ssq[0:1, tt * NT:(tt + 1) * NT], self.onesB[:],
                                                sq[p][:, tt * NT:(tt + 1) * NT],
                                                start=(i == 0), stop=(i == NCD - 1))
                            return inst
                        gw = self.guard() if ic == 0 else []
                        self.emit("pe", mk_stats,
                                  waits=[(f"ld_x{par}", ldv), ("act", actv), ("gp", 4)] + gw,
                                  incs=[("pe", 1)])
                        xs_free[par] = [("pe", self.c("pe"))]
                        sq_free[par] = [("pe", self.c("pe"))]
                    rows_done = self.ln_rows(ps_sum, ps_ssq, TSUP, self.rows["r"], self.rows["mr"])

                with ExitStack() as pctx:
                    psR = pctx.enter_context(nc.psum_tensor(self.u("psR"), [128, TSUP], F32))
                    psMR = pctx.enter_context(nc.psum_tensor(self.u("psMR"), [128, TSUP], F32))
                    bc_done = self.bcast(psR, psMR, self.rows["r"], self.rows["mr"], 0, NTT,
                                         waits=[("dve", rows_done)])
                    for ic in range(NCD):
                        par = ic % 2
                        self.emit("sy",
                                  lambda e, p=par, i=ic, ts0=ts0: e.dma_start(
                                      xs[p][:], self.xT[i * 128:(i + 1) * 128, ts0:ts0 + TSUP]),
                                  waits=xs_free[par], incs=[(f"ld_x{par}", 16)])
                        ldv = self.c(f"ld_x{par}")
                        self.emit("dve", lambda e, p=par, psR=psR: e.tensor_mul(
                            xs[p][:], xs[p][:], psR[:]),
                            waits=[(f"ld_x{par}", ldv), ("pe", bc_done)], incs=[("dve", 1)])
                        self.emit("dve", lambda e, p=par, i=ic, psMR=psMR: e.tensor_sub(
                            lnb[:, i * TSUP:(i + 1) * TSUP], xs[p][:], psMR[:]),
                            waits=[("dve", self.c("dve"))], incs=[("dve", 1)])
                        xs_free[par] = [("dve", self.c("dve"))]
                    ln_ready = self.c("dve")

                with ExitStack() as pctx:
                    psP = [pctx.enter_context(nc.psum_tensor(self.u(f"psP{i}"), [128, TSUP], F32))
                           for i in range(2)]
                    psP_free = [[], []]
                    for oc in range(NCD):
                        par = oc % 2

                        def mk_kproj(e, o=oc, p=par, psP=psP):
                            for ic in range(NCD):
                                for tt in range(NTT):
                                    inst = e.matmul(
                                        psP[p][:, tt * NT:(tt + 1) * NT],
                                        wkb[:, ic * D + o * 128: ic * D + o * 128 + 128],
                                        lnb[:, ic * TSUP + tt * NT: ic * TSUP + (tt + 1) * NT],
                                        start=(ic == 0), stop=(ic == NCD - 1))
                            return inst
                        self.emit("pe", mk_kproj,
                                  waits=[("dve", ln_ready), ("ld_w0", wproj_ready)] + psP_free[par],
                                  incs=[("pe", 1)])
                        mmv = self.c("pe")
                        self.emit("dve", lambda e, p=par, psP=psP: e.tensor_copy(kvo[p][:], psP[p][:]),
                                  waits=[("pe", mmv)] + kvo_free[par], incs=[("dve", 1)])
                        psP_free[par] = [("dve", self.c("dve"))]
                        dvv = self.c("dve")
                        self.emit("gp", lambda e, p=par, o=oc, ts0=ts0: e.dma_start(
                            self.K_sc[o, :, ts0:ts0 + TSUP], kvo[p][:]),
                            waits=[("dve", dvv)], incs=[(f"st_k{par}", 16)])
                        kvo_free[par] = [(f"st_k{par}", self.c(f"st_k{par}"))]
                    if sup == 0:
                        NTTQ = T // NT
                        for oc in range(NCD):
                            par = oc % 2

                            def mk_qproj(e, o=oc, p=par, psP=psP):
                                for ic in range(NCD):
                                    for tt in range(NTTQ):
                                        inst = e.matmul(
                                            psP[p][:, tt * NT:(tt + 1) * NT],
                                            wqb[:, ic * D + o * 128: ic * D + o * 128 + 128],
                                            lnb[:, ic * TSUP + tt * NT: ic * TSUP + (tt + 1) * NT],
                                            start=(ic == 0), stop=(ic == NCD - 1))
                                return inst
                            self.emit("pe", mk_qproj,
                                      waits=[("dve", ln_ready), ("ld_w0", wproj_ready)] + psP_free[par],
                                      incs=[("pe", 1)])
                            mmv = self.c("pe")
                            self.emit("dve", lambda e, p=par, o=oc, psP=psP: e.tensor_copy(
                                self.qbuf[:, o * T:(o + 1) * T], psP[p][:, 0:T]),
                                waits=[("pe", mmv)], incs=[("dve", 1)])
                            psP_free[par] = [("dve", self.c("dve"))]

                with ExitStack() as pctx:
                    psV = [pctx.enter_context(nc.psum_tensor(self.u(f"psV{i}"), [128, D], F32))
                           for i in range(2)]
                    psV_free = [self.guard(), self.guard()]
                    for tk in range(TSUP // 128):
                        par = tk % 2

                        def mk_vproj(e, t=tk, p=par, psV=psV):
                            for ic in range(NCD):
                                for oh in range(2):
                                    inst = e.matmul(
                                        psV[p][:, oh * NT:(oh + 1) * NT],
                                        lnb[:, ic * TSUP + t * 128: ic * TSUP + t * 128 + 128],
                                        wvb[:, ic * D + oh * NT: ic * D + (oh + 1) * NT],
                                        start=(ic == 0), stop=(ic == NCD - 1))
                            return inst
                        self.emit("pe", mk_vproj,
                                  waits=[("dve", ln_ready), ("ld_w0", wproj_ready)] + psV_free[par],
                                  incs=[("pe", 1)])
                        mmv = self.c("pe")
                        def mk_vevac(e, p=par, psV=psV):
                            dst = bass.AP(vvo[p], 0, [[NH * (DH + 1), 128], [DH + 1, NH], [1, DH]])
                            return e.tensor_copy(dst, psV[p][:])
                        self.emit("dve", mk_vevac,
                                  waits=[("pe", mmv), ("gp", self.c("gp"))] + vvo_free[par],
                                  incs=[("dve", 1)])
                        psV_free[par] = [("dve", self.c("dve"))]
                        dvv = self.c("dve")
                        tkg = (ts0 + tk * 128) // 128
                        NKC_ALL = self.S // 128

                        def mk_vst(e, p=par, tkg=tkg, NKC_ALL=NKC_ALL):
                            dst = bass.AP(self.V_sc.tensor, tkg * (DH + 1),
                                          [[NKC_ALL * (DH + 1), 128],
                                           [128 * NKC_ALL * (DH + 1), NH], [1, DH + 1]])
                            src = bass.AP(vvo[p], 0,
                                          [[NH * (DH + 1), 128], [DH + 1, NH], [1, DH + 1]])
                            return e.dma_start(dst, src)
                        self.emit("gp", mk_vst, waits=[("dve", dvv)], incs=[(f"st_v{par}", 16)])
                        vvo_free[par] = [(f"st_v{par}", self.c(f"st_v{par}"))]

    def alloc_rows(self, scope, W, tag):
        nc = self.nc
        r = {}
        for n in ("m", "v", "r", "mr"):
            r[n] = scope.enter_context(nc.sbuf_tensor(f"row_{tag}_{n}", [1, W], F32))
        self.rows = r

    def ln_rows(self, ps_sum, ps_ssq, W, rdst, mrdst, roff=0, rdst_free=()):
        pev = self.c("pe")
        ro = self.rows
        self.emit("dve", lambda e, ro=ro: e.tensor_scalar_mul(
            ro["m"][:, 0:W], ps_sum[:], 1.0 / D),
            waits=[("pe", pev)], incs=[("dve", 1)])
        self.emit("dve", lambda e, ro=ro: e.tensor_scalar(
            ro["v"][:, 0:W], ps_ssq[:], 1.0 / D, EPS_LN, ALU.mult, ALU.add),
            waits=[("pe", pev)], incs=[("dve", 1)])
        d0 = self.c("dve")
        self.emit("dve", lambda e, ro=ro: e.tensor_mul(
            rdst[:, roff:roff + W], ro["m"][:, 0:W], ro["m"][:, 0:W]),
            waits=[("dve", d0)] + list(rdst_free), incs=[("dve", 1)])
        self.emit("dve", lambda e, ro=ro: e.tensor_sub(
            ro["v"][:, 0:W], ro["v"][:, 0:W], rdst[:, roff:roff + W]),
            waits=[("dve", self.c("dve"))], incs=[("dve", 1)])
        dv = self.c("dve")
        self.emit("act", lambda e, ro=ro: e.activation(
            ro["v"][:, 0:W], ro["v"][:, 0:W], AF.Ln),
            waits=[("dve", dv)], incs=[("act", 1)])
        self.emit("act", lambda e, ro=ro: e.activation(
            rdst[:, roff:roff + W], ro["v"][:, 0:W], AF.Exp, scale=-0.5),
            waits=[("act", self.c("act"))], incs=[("act", 1)])
        av = self.c("act")
        self.emit("dve", lambda e, ro=ro: e.tensor_mul(
            mrdst[:, roff:roff + W], ro["m"][:, 0:W], rdst[:, roff:roff + W]),
            waits=[("act", av)], incs=[("dve", 1)])
        return self.c("dve")

    def bcast(self, psR, psMR, rsrc, mrsrc, roff, ntt, waits):
        def mk(e):
            for tt in range(ntt):
                e.matmul(psR[:, tt * NT:(tt + 1) * NT], self.onesR[:],
                         rsrc[0:1, roff + tt * NT: roff + (tt + 1) * NT], start=True, stop=True)
            for tt in range(ntt):
                inst = e.matmul(psMR[:, tt * NT:(tt + 1) * NT], self.onesR[:],
                                mrsrc[0:1, roff + tt * NT: roff + (tt + 1) * NT], start=True, stop=True)
            return inst
        self.emit("pe", mk, waits=list(waits) + self.guard(), incs=[("pe", 1)])
        return self.c("pe")

    def open_p1(self, sc):
        nc, S = self.nc, self.S
        self.vbuf = [sc.enter_context(nc.sbuf_tensor(f"vbuf{i}", [128, 8 * (DH + 1)], BF16))
                     for i in range(4)]
        self.rs_row = [sc.enter_context(nc.sbuf_tensor(f"rs_row{i}", [1, 512], F32))
                       for i in range(2)]
        self.sig_pre = sc.enter_context(nc.sbuf_tensor("sig_pre", [1, NT], BF16))
        self.rs_free = [[], []]
        self.vbuf_free = [[], [], [], []]

    def gen_phase1(self, qms, kct, psS, psO):
        nc, S, T = self.nc, self.S, self.T
        QM = 512
        NKC = S // 128
        tiles = []
        k = 0
        while k < NKC:
            n = min(kct, NKC - k)
            tiles.append((k, n))
            k += n
        NKC_ALL = NKC
        kbuf, vbuf, wtil, rs_row = self.kbuf, self.vbuf, self.wtil, self.rs_row
        kbuf_free, rs_free = self.kbuf_free, self.rs_free
        vbuf_free, wtil_free = self.vbuf_free, self.wtil_free

        if True:
            psS_free = [self.guard(), self.guard()]
            psO_free = [self.guard(), self.guard()]

            self.emit("act", lambda e: e.activation(
                self.sig_pre[:], self.warmsrc[0:1, 0:NT], AF.Sigmoid),
                waits=[("gp", 1), ("bar", self.bar_n)], incs=[("act", 1)])
            first = [True]
            for qm in qms:
                q0 = qm * QM
                for hp in range(NCD):
                    kpar = hp % len(kbuf)
                    SP4 = S // 4
                    k_parts = []
                    self.emit("sy", lambda e, h=hp, p=kpar, SP4=SP4: e.dma_start(
                        kbuf[p][:, 0:SP4], self.K_sc[h, :, 0:SP4]),
                        waits=kbuf_free[kpar], incs=[(f"ld_k{kpar}p0", 16)])
                    k_parts.append((f"ld_k{kpar}p0", self.c(f"ld_k{kpar}p0")))

                    def emit_k_rest(hp=hp, kpar=kpar, SP4=SP4, k_parts=k_parts):
                        for pt in range(1, 4):
                            self.emit("sy", lambda e, h=hp, p=kpar, pt=pt, SP4=SP4: e.dma_start(
                                kbuf[p][:, pt * SP4:(pt + 1) * SP4],
                                self.K_sc[h, :, pt * SP4:(pt + 1) * SP4]),
                                waits=kbuf_free[kpar], incs=[(f"ld_k{kpar}p{pt}", 16)])
                            k_parts.append((f"ld_k{kpar}p{pt}", self.c(f"ld_k{kpar}p{pt}")))
                    k_ld = None
                    if first[0]:
                        first[0] = False

                        def mk_warm(e):
                            for _ in range(16):
                                inst = e.matmul(psS[1][:, 0:NT], self.warmsrc[:, 0:128],
                                                self.warmsrc[:, 128:640],
                                                start=True, stop=True)
                            return inst
                        self.emit("pe", mk_warm,
                                  waits=[("gp", 1), k_parts[0]] + psS_free[1],
                                  incs=[("pe", 1)])
                        psS_free[1] = [("pe", self.c("pe"))]
                    pending_pv = None
                    v_lds = {}
                    for ti, (kc0, nk) in enumerate(tiles):
                        if ti == 1:
                            emit_k_rest()
                        for kc in range(kc0, kc0 + nk):
                            if kc % 8 == 0:
                                kb = kc // 8
                                for hh in range(2):
                                    vslot = hh * 2 + kb % 2
                                    self.emit("sy", lambda e, hh_=2 * hp + hh, kb=kb, p=vslot:
                                              e.dma_start(vbuf[p][:], bass.AP(
                                                  self.V_sc.tensor,
                                                  hh_ * 128 * NKC_ALL * (DH + 1) + kb * 8 * (DH + 1),
                                                  [[NKC_ALL * (DH + 1), 128],
                                                   [1, 8 * (DH + 1)]])),
                                              waits=vbuf_free[vslot],
                                              incs=[(f"ld_v{vslot}", 16)])
                                    v_lds[(hh, kb)] = self.c(f"ld_v{vslot}")

                        sig_done = [0, 0]
                        tp_ = ti % 2
                        for hh in range(2):
                            def mk_sc(e, kp=kpar, k0=kc0, n=nk, hp_=hp, q0=q0, hh=hh):
                                b = hh * 64
                                for j in range(n):
                                    inst = e.matmul(
                                        psS[hh][:, j * QM:(j + 1) * QM],
                                        kbuf[kp][b:b + 64, (k0 + j) * 128:(k0 + j + 1) * 128],
                                        self.qbuf[b:b + 64,
                                                  hp_ * T + q0: hp_ * T + q0 + QM],
                                        start=True, stop=True)
                                return inst
                            kpart_needed = min(((kc0 + nk - 1) * 128) // (S // 4),
                                               len(k_parts) - 1)
                            self.emit("pe", mk_sc,
                                      waits=[k_parts[kpart_needed]] + psS_free[hh],
                                      incs=[("pe", 1)])
                            sc_done = self.c("pe")
                            w = hh * 2 + tp_
                            self.emit("act", lambda e, w=w, hh=hh, n=nk: e.activation(
                                wtil[w][:, 0:n * QM], psS[hh][:, 0:n * QM],
                                AF.Sigmoid, scale=ATT_SCALE),
                                waits=[("pe", sc_done)] + wtil_free[w],
                                incs=[("act", 1)])
                            sig_done[hh] = self.c("act")
                            psS_free[hh] = [("act", sig_done[hh])]

                        if pending_pv is not None:
                            pending_pv()

                        vwaits = [[], []]
                        for hh in range(2):
                            for kc in range(kc0, kc0 + nk):
                                kb = kc // 8
                                vwaits[hh].append((f"ld_v{hh * 2 + kb % 2}", v_lds[(hh, kb)]))
                            vwaits[hh] = sorted(set(vwaits[hh]))

                        def mk_pv_emit(k0=kc0, n=nk, sig=tuple(sig_done), tp_=tp_,
                                       vw=(tuple(vwaits[0]), tuple(vwaits[1]))):
                            for hh in range(2):
                                w = hh * 2 + tp_
                                def mk_pv(e, hh=hh, w=w):
                                    for j in range(n):
                                        k = k0 + j
                                        a_ = k % 8
                                        vslot = hh * 2 + (k // 8) % 2
                                        inst = e.matmul(
                                            psO[hh][:, :],
                                            vbuf[vslot][:, a_ * (DH + 1): a_ * (DH + 1) + DH + 1],
                                            wtil[w][:, j * QM:(j + 1) * QM],
                                            start=(k == 0), stop=(k == NKC - 1))
                                    return inst
                                pvw = [("act", sig[hh])] + list(vw[hh])
                                if k0 == 0:
                                    pvw += psO_free[hh]
                                self.emit("pe", mk_pv, waits=pvw, incs=[("pe", 1)])
                                wtil_free[w] = [("pe", self.c("pe"))]
                                for kc in range(k0, k0 + n):
                                    if kc % 8 == 7 or kc == NKC - 1:
                                        vbuf_free[hh * 2 + (kc // 8) % 2] = [("pe", self.c("pe"))]
                        pending_pv = mk_pv_emit
                        yield
                    pending_pv()
                    pending_pv = None
                    pv_all = self.c("pe")
                    for hh in range(2):
                        h = 2 * hp + hh
                        self.emit("dve", lambda e, hh=hh: e.tensor_copy(
                            rs_row[hh][:], psO[hh][64:65, :]),
                            waits=[("pe", pv_all)] + rs_free[hh], incs=[("dve", 1)])
                        rsd = self.c("dve")
                        self.emit("sy", lambda e, hh=hh, h=h, q=q0: e.dma_start(
                            self.RS_sc[h:h + 1, q:q + QM], rs_row[hh][:]),
                            waits=[("dve", rsd)], incs=[(f"st_r{hh}", 16)])
                        rs_free[hh] = [(f"st_r{hh}", self.c(f"st_r{hh}"))]
                        self.emit("dve", lambda e, hh=hh, c=hp, q=q0: e.tensor_copy(
                            self.obuf[hh * 64:hh * 64 + 64, c * T + q: c * T + q + QM],
                            psO[hh][0:64, :]),
                            waits=[("pe", pv_all)], incs=[("dve", 1)])
                        psO_free[hh] = [("dve", self.c("dve"))]
                    yield
                    kbuf_free[kpar] = [("pe", self.c("pe"))]

    def phase2(self, t0, TW, ln2t):
        nc, T = self.nc, self.T
        NTT = TW // NT

        self.warmup()
        with ExitStack() as sc:
            wob = sc.enter_context(nc.sbuf_tensor(self.u("wob"), [128, NCD * D], BF16))
            abuf = sc.enter_context(nc.sbuf_tensor(self.u("abuf"), [128, NCD * TW], BF16, side="right"))
            Eb = sc.enter_context(nc.sbuf_tensor(self.u("Eb"), [16, D], F32))
            RSm = sc.enter_context(nc.sbuf_tensor(self.u("RSm"), [16, TW], F32))
            self.alloc_rows(sc, TW, self.u("p2"))
            r2loc = sc.enter_context(nc.sbuf_tensor(self.u("r2loc"), [1, TW], F32))
            mr2loc = sc.enter_context(nc.sbuf_tensor(self.u("mr2loc"), [1, TW], F32))
            self.r2row, self.mr2row = r2loc, mr2loc
            sq2 = [sc.enter_context(nc.sbuf_tensor(self.u(f"sq2{i}"), [128, TW], BF16)) for i in range(2)]
            xb2 = [sc.enter_context(nc.sbuf_tensor(self.u(f"xb2{i}"), [128, TW], F32)) for i in range(2)]
            y1t = [sc.enter_context(nc.sbuf_tensor(self.u(f"y1t{i}"), [128, TW], F32)) for i in range(2)]

            self.emit("gp", lambda e: e.dma_start(wob[:], self.wo),
                      waits=[("bar", self.bar_n)], incs=[("ld_w1", 16)])
            wo_ready = self.c("ld_w1")

            self.emit("sy", lambda e, RSm=RSm, t0=t0: e.dma_start(
                RSm[:], self.RS_sc[:, t0:t0 + TW]), incs=[("st_rs", 16)])
            self.emit("sy", lambda e, Eb=Eb: e.dma_start(Eb[:], self.Esel),
                      incs=[("st_rs", 16)])
            rsv = self.c("st_rs")
            self.emit("dve", lambda e, RSm=RSm: e.tensor_scalar_add(RSm[:], RSm[:], EPS_ATT),
                      waits=[("st_rs", rsv)], incs=[("dve", 1)])
            self.emit("dve", lambda e, RSm=RSm: e.reciprocal(RSm[:], RSm[:]),
                      waits=[("dve", self.c("dve"))], incs=[("dve", 1)])
            rs_done = self.c("dve")

            with ExitStack() as pctx:
                psB = [pctx.enter_context(nc.psum_tensor(self.u(f"psB{i}"), [128, TW], F32))
                       for i in range(2)]
                psB_free = [self.guard(), self.guard()]
                for c in range(NCD):
                    par = c % 2

                    def mk_b(e, c=c, p=par, Eb=Eb, RSm=RSm, psB=psB):
                        for tt in range(NTT):
                            inst = e.matmul(psB[p][:, tt * NT:(tt + 1) * NT],
                                            Eb[:, c * 128:(c + 1) * 128],
                                            RSm[0:16, tt * NT:(tt + 1) * NT],
                                            start=True, stop=True)
                        return inst
                    self.emit("pe", mk_b, waits=[("dve", rs_done)] + psB_free[par],
                              incs=[("pe", 1)])
                    bv = self.c("pe")
                    self.emit("dve", lambda e, c=c, p=par, psB=psB, t0=t0: e.tensor_mul(
                        self.obuf[:, c * T + t0: c * T + t0 + TW],
                        self.obuf[:, c * T + t0: c * T + t0 + TW], psB[p][:]),
                        waits=[("pe", bv)], incs=[("dve", 1)])
                    psB_free[par] = [("dve", self.c("dve"))]
                onorm_done = self.c("dve")

            with ExitStack() as pctx:
                psA = [pctx.enter_context(nc.psum_tensor(self.u(f"psA{i}"), [128, TW], F32))
                       for i in range(2)]
                psA_free = [[], []]
                for oc in range(NCD):
                    par = oc % 2

                    def mk_wo(e, o=oc, p=par, psA=psA, wob=wob, t0=t0):
                        for ic in range(NCD):
                            for tt in range(NTT):
                                inst = e.matmul(
                                    psA[p][:, tt * NT:(tt + 1) * NT],
                                    wob[:, ic * D + o * 128: ic * D + o * 128 + 128],
                                    self.obuf[:, ic * T + t0 + tt * NT: ic * T + t0 + (tt + 1) * NT],
                                    start=(ic == 0), stop=(ic == NCD - 1))
                        return inst
                    self.emit("pe", mk_wo,
                              waits=[("dve", onorm_done), ("ld_w1", wo_ready)] + psA_free[par],
                              incs=[("pe", 1)])
                    mmv = self.c("pe")
                    self.emit("dve", lambda e, o=oc, p=par, psA=psA, abuf=abuf: e.tensor_copy(
                        abuf[:, o * TW:(o + 1) * TW], psA[p][:]),
                        waits=[("pe", mmv)], incs=[("dve", 1)])
                    psA_free[par] = [("dve", self.c("dve"))]
                a_done = self.c("dve")

            with ExitStack() as pctx:
                ps_sum = pctx.enter_context(nc.psum_tensor(self.u("ps_sum2"), [1, TW], F32))
                ps_ssq = pctx.enter_context(nc.psum_tensor(self.u("ps_ssq2"), [1, TW], F32))
                sq_free = [[], []]
                for ic in range(NCD):
                    par = ic % 2
                    self.emit("act", lambda e, i=ic, p=par, abuf=abuf, sq2=sq2: e.activation(
                        sq2[p][:], abuf[:, i * TW:(i + 1) * TW], AF.Square),
                        waits=[("dve", a_done)] + sq_free[par], incs=[("act", 1)])
                    actv = self.c("act")

                    def mk_stats(e, i=ic, p=par, abuf=abuf, sq2=sq2, ps_sum=ps_sum, ps_ssq=ps_ssq):
                        for tt in range(NTT):
                            e.matmul(ps_sum[0:1, tt * NT:(tt + 1) * NT], self.onesB[:],
                                     abuf[:, i * TW + tt * NT: i * TW + (tt + 1) * NT],
                                     start=(i == 0), stop=(i == NCD - 1))
                        for tt in range(NTT):
                            inst = e.matmul(ps_ssq[0:1, tt * NT:(tt + 1) * NT], self.onesB[:],
                                            sq2[p][:, tt * NT:(tt + 1) * NT],
                                            start=(i == 0), stop=(i == NCD - 1))
                        return inst
                    gw = self.guard() if ic == 0 else []
                    self.emit("pe", mk_stats, waits=[("act", actv), ("dve", a_done)] + gw,
                              incs=[("pe", 1)])
                    sq_free[par] = [("pe", self.c("pe"))]
                rows_done = self.ln_rows(ps_sum, ps_ssq, TW, self.rows["r"], self.rows["mr"])

            xb_free = [[], []]
            y1_free = [[], []]
            with ExitStack() as pctx:
                psR = pctx.enter_context(nc.psum_tensor(self.u("psRy"), [128, TW], F32))
                psMR = pctx.enter_context(nc.psum_tensor(self.u("psMRy"), [128, TW], F32))
                ps2s = pctx.enter_context(nc.psum_tensor(self.u("ps2s"), [1, TW], F32))
                ps2q = pctx.enter_context(nc.psum_tensor(self.u("ps2q"), [1, TW], F32))
                bc_done = self.bcast(psR, psMR, self.rows["r"], self.rows["mr"], 0, NTT,
                                     waits=[("dve", rows_done)])
                for ic in range(NCD):
                    par = ic % 2
                    self.emit("sy", lambda e, i=ic, p=par, t0=t0, xb2=xb2: e.dma_start(
                        xb2[p][:], self.xT[i * 128:(i + 1) * 128, t0:t0 + TW]),
                        waits=xb_free[par], incs=[(f"ld_x{par}", 16)])
                    ldv = self.c(f"ld_x{par}")
                    self.emit("dve", lambda e, i=ic, p=par, psR=psR, abuf=abuf, y1t=y1t: e.tensor_mul(
                        y1t[p][:], abuf[:, i * TW:(i + 1) * TW], psR[:]),
                        waits=[("pe", bc_done)] + y1_free[par], incs=[("dve", 1)])
                    self.emit("dve", lambda e, p=par, psMR=psMR, y1t=y1t: e.tensor_sub(
                        y1t[p][:], y1t[p][:], psMR[:]),
                        waits=[("dve", self.c("dve"))], incs=[("dve", 1)])
                    self.emit("dve", lambda e, p=par, xb2=xb2, y1t=y1t: e.tensor_add(
                        y1t[p][:], y1t[p][:], xb2[p][:]),
                        waits=[("dve", self.c("dve")), (f"ld_x{par}", ldv)],
                        incs=[("dve", 1)])
                    dvv = self.c("dve")
                    xb_free[par] = [("dve", dvv)]
                    self.emit("gp", lambda e, i=ic, p=par, t0=t0, y1t=y1t: e.dma_start(
                        self.y1_sc[i, :, t0:t0 + TW], y1t[p][:]),
                        waits=[("dve", dvv)], incs=[(f"st_y{par}", 16)])
                    self.emit("act", lambda e, p=par, sq2=sq2, y1t=y1t: e.activation(
                        sq2[p][:, 0:TW], y1t[p][:], AF.Square),
                        waits=[("dve", dvv)], incs=[("act", 1)])
                    actv = self.c("act")

                    def mk_stats2(e, i=ic, p=par, ps2s=ps2s, ps2q=ps2q, sq2=sq2, y1t=y1t):
                        for tt in range(NTT):
                            e.matmul(ps2s[0:1, tt * NT:(tt + 1) * NT], self.onesF[:],
                                     y1t[p][:, tt * NT:(tt + 1) * NT],
                                     start=(i == 0), stop=(i == NCD - 1))
                        for tt in range(NTT):
                            inst = e.matmul(ps2q[0:1, tt * NT:(tt + 1) * NT], self.onesB[:],
                                            sq2[p][:, tt * NT:(tt + 1) * NT],
                                            start=(i == 0), stop=(i == NCD - 1))
                        return inst
                    self.emit("pe", mk_stats2, waits=[("act", actv)], incs=[("pe", 1)])
                    y1_free[par] = [("pe", self.c("pe")),
                                    (f"st_y{par}", self.c(f"st_y{par}"))]
                rows2_done = self.ln_rows(ps2s, ps2q, TW, r2loc, mr2loc, roff=0)

            with ExitStack() as pctx:
                psR = pctx.enter_context(nc.psum_tensor(self.u("psR2a"), [128, TW], F32))
                psMR = pctx.enter_context(nc.psum_tensor(self.u("psMR2a"), [128, TW], F32))
                bc_done = self.bcast(psR, psMR, r2loc, mr2loc, 0, NTT,
                                     waits=[("dve", rows2_done)])
                for ic in range(NCD):
                    par = ic % 2
                    self.emit("sy", lambda e, i=ic, p=par, t0=t0, y1t=y1t: e.dma_start(
                        y1t[p][:], self.y1_sc[i, :, t0:t0 + TW]),
                        waits=y1_free[par], incs=[(f"ld_y{par}", 16)])
                    ldv = self.c(f"ld_y{par}")
                    self.emit("dve", lambda e, p=par, psR=psR, y1t=y1t: e.tensor_mul(
                        y1t[p][:], y1t[p][:], psR[:]),
                        waits=[(f"ld_y{par}", ldv), ("pe", bc_done)], incs=[("dve", 1)])
                    self.emit("dve", lambda e, i=ic, p=par, psMR=psMR, ln2t=ln2t, y1t=y1t: e.tensor_sub(
                        ln2t[:, i * TW:(i + 1) * TW], y1t[p][:], psMR[:]),
                        waits=[("dve", self.c("dve"))], incs=[("dve", 1)])
                    y1_free[par] = [("dve", self.c("dve"))]

    def gen_phase3(self, t0, th, ln2t, ln2_base, ln2_w, inline_stats=False):
        nc = self.nc
        NTT = th // NT
        off = t0 - ln2_base

        with ExitStack() as sc:
            fb1 = sc.enter_context(nc.sbuf_tensor(self.u("fb1"), [128, NCH * th], BF16))
            wfb = [sc.enter_context(nc.sbuf_tensor(self.u(f"wfb{i}"), [128, NCH * 128], BF16))
                   for i in range(2)]
            y1t = [sc.enter_context(nc.sbuf_tensor(self.u(f"y1f{i}"), [128, th], F32)) for i in range(2)]
            outb = [sc.enter_context(nc.sbuf_tensor(self.u(f"outb{i}"), [128, th], F32)) for i in range(2)]
            sq3 = [sc.enter_context(nc.sbuf_tensor(self.u(f"sq3{i}"), [128, th], BF16)) for i in range(2)]

            wfb_free = [self.guard(), self.guard()]
            y1_free = [self.guard(), self.guard()]
            out_free = [[(f"st_o0", self.c("st_o0"))], [(f"st_o1", self.c("st_o1"))]]
            wk = [0]

            def stream_w(src_ap, width):
                k = wk[0]
                b = k % 2
                self.emit("gp", lambda e, s=src_ap, b=b, w=width: e.dma_start(
                    wfb[b][:, 0:w], s), waits=wfb_free[b], incs=[(f"ld_w{b}", 16)])
                wk[0] += 1
                return b, self.c(f"ld_w{b}")

            ln2_ready = [("dve", self.c("dve"))]

            with ExitStack() as pctx:
                psF = [pctx.enter_context(nc.psum_tensor(self.u(f"psF{i}"), [128, th], F32))
                       for i in range(2)]
                psF_free = [self.guard(), self.guard()]
                for stage_i, (wsrc, scl) in enumerate([(self.w1, self.s1b),
                                                       (self.w2, self.s2b)]):
                    for oc in range(NCH):
                        b, wld = stream_w(wsrc[oc], NCD * 128)
                        par = oc % 2

                        def mk_mm(e, b=b, p=par, psF=psF):
                            for ic in range(NCD):
                                for tt in range(NTT):
                                    inst = e.matmul(
                                        psF[p][:, tt * NT:(tt + 1) * NT],
                                        wfb[b][:, ic * 128:(ic + 1) * 128],
                                        ln2t[:, ic * ln2_w + off + tt * NT: ic * ln2_w + off + (tt + 1) * NT],
                                        start=(ic == 0), stop=(ic == NCD - 1))
                            return inst
                        self.emit("pe", mk_mm,
                                  waits=ln2_ready + [(f"ld_w{b}", wld)] + psF_free[par],
                                  incs=[("pe", 1)])
                        mmv = self.c("pe")
                        wfb_free[b] = [("pe", mmv)]
                        if stage_i == 0:
                            self.emit("act", lambda e, o=oc, p=par, psF=psF, scl=scl: e.activation(
                                fb1[:, o * th:(o + 1) * th], psF[p][:], AF.Silu,
                                scale=scl[:, o:o + 1]),
                                waits=[("pe", mmv)], incs=[("act", 1)])
                            psF_free[par] = [("act", self.c("act"))]
                        else:
                            self.emit("dve", lambda e, o=oc, p=par, psF=psF, scl=scl: e.scalar_tensor_tensor(
                                fb1[:, o * th:(o + 1) * th], psF[p][:], scl[:, o:o + 1],
                                fb1[:, o * th:(o + 1) * th], ALU.mult, ALU.mult),
                                waits=[("pe", mmv), ("act", self.c("act"))],
                                incs=[("dve", 1)])
                            psF_free[par] = [("dve", self.c("dve"))]
                        yield

            if True:
                pctx = ExitStack()
                fb2 = pctx.enter_context(nc.sbuf_tensor(self.u("fb2"), [128, NCH * th], BF16))
                psF = [pctx.enter_context(nc.psum_tensor(self.u(f"psG{i}"), [128, th], F32))
                       for i in range(2)]
                psF_free = [self.guard(), self.guard()]
                for stage_i, (wsrc, scl, srcb, dstb) in enumerate([
                        (self.h1, self.hs1b, fb1, fb2),
                        (self.h2, self.hs2b, fb2, fb1)]):
                    src_ready = [("dve", self.c("dve")), ("act", self.c("act"))]
                    for oc in range(NCH):
                        b, wld = stream_w(wsrc[oc], NCH * 128)
                        par = oc % 2
                        for half in range(2):
                            ic_lo, ic_hi = half * 16, half * 16 + 16

                            def mk_mm(e, b=b, p=par, s=srcb, psF=psF, lo=ic_lo, hi=ic_hi):
                                for ic in range(lo, hi):
                                    for tt in range(NTT):
                                        inst = e.matmul(
                                            psF[p][:, tt * NT:(tt + 1) * NT],
                                            wfb[b][:, ic * 128:(ic + 1) * 128],
                                            s[:, ic * th + tt * NT: ic * th + (tt + 1) * NT],
                                            start=(ic == 0), stop=(ic == NCH - 1))
                                return inst
                            w_ = src_ready + [(f"ld_w{b}", wld)]
                            if half == 0:
                                w_ = w_ + psF_free[par]
                            self.emit("pe", mk_mm, waits=w_, incs=[("pe", 1)])
                            yield
                        mmv = self.c("pe")
                        wfb_free[b] = [("pe", mmv)]
                        self.emit("act", lambda e, o=oc, p=par, d=dstb, sc_=scl, psF=psF: e.activation(
                            d[:, o * th:(o + 1) * th], psF[p][:], AF.Gelu,
                            scale=sc_[:, o:o + 1]),
                            waits=[("pe", mmv)], incs=[("act", 1)])
                        psF_free[par] = [("act", self.c("act"))]
                pctx.close()

                with ExitStack() as p2ctx:
                    fbf = p2ctx.enter_context(nc.sbuf_tensor(self.u("fbf"), [128, NCD * th], BF16))
                    self.alloc_rows(p2ctx, th, self.u("p3"))
                    with ExitStack() as p3ctx:
                        psW = [p3ctx.enter_context(nc.psum_tensor(self.u(f"psW{i}"), [128, th], F32))
                               for i in range(2)]
                        if inline_stats:
                            ps3s = p3ctx.enter_context(nc.psum_tensor(self.u("ps3s"), [1, th], F32))
                            ps3q = p3ctx.enter_context(nc.psum_tensor(self.u("ps3q"), [1, th], F32))
                            sq_free = [[], []]
                        psW_free = [self.guard(), self.guard()]
                        src_ready = [("act", self.c("act"))]
                        for oc in range(NCD):
                            b, wld = stream_w(self.w3[oc], NCH * 128)
                            par = oc % 2
                            for half in range(2):
                                ic_lo, ic_hi = half * 16, half * 16 + 16

                                def mk_mm(e, b=b, p=par, psF=psW, lo=ic_lo, hi=ic_hi):
                                    for ic in range(lo, hi):
                                        for tt in range(NTT):
                                            inst = e.matmul(
                                                psF[p][:, tt * NT:(tt + 1) * NT],
                                                wfb[b][:, ic * 128:(ic + 1) * 128],
                                                fb1[:, ic * th + tt * NT: ic * th + (tt + 1) * NT],
                                                start=(ic == 0), stop=(ic == NCH - 1))
                                    return inst
                                w_ = src_ready + [(f"ld_w{b}", wld)]
                                if half == 0:
                                    w_ = w_ + psW_free[par]
                                self.emit("pe", mk_mm, waits=w_, incs=[("pe", 1)])
                                yield
                            mmv = self.c("pe")
                            wfb_free[b] = [("pe", mmv)]
                            self.emit("dve", lambda e, o=oc, p=par, psF=psW, fbf=fbf: e.tensor_scalar_mul(
                                fbf[:, o * th:(o + 1) * th], psF[p][:], self.s3b[:, o:o + 1]),
                                waits=[("pe", mmv)], incs=[("dve", 1)])
                            psW_free[par] = [("dve", self.c("dve"))]
                            if inline_stats:
                                fv = self.c("dve")
                                self.emit("act", lambda e, o=oc, p=par, fbf=fbf, sq3=sq3: e.activation(
                                    sq3[p][:], fbf[:, o * th:(o + 1) * th], AF.Square),
                                    waits=[("dve", fv)] + sq_free[par], incs=[("act", 1)])
                                actv = self.c("act")

                                def mk_stats(e, o=oc, p=par, ps3s=ps3s, ps3q=ps3q, fbf=fbf, sq3=sq3):
                                    for tt in range(NTT):
                                        e.matmul(ps3s[0:1, tt * NT:(tt + 1) * NT], self.onesB[:],
                                                 fbf[:, o * th + tt * NT: o * th + (tt + 1) * NT],
                                                 start=(o == 0), stop=(o == NCD - 1))
                                    for tt in range(NTT):
                                        inst = e.matmul(ps3q[0:1, tt * NT:(tt + 1) * NT], self.onesB[:],
                                                        sq3[p][:, tt * NT:(tt + 1) * NT],
                                                        start=(o == 0), stop=(o == NCD - 1))
                                    return inst
                                gw = self.guard() if oc == 0 else []
                                self.emit("pe", mk_stats, waits=[("act", actv), ("dve", fv)] + gw,
                                          incs=[("pe", 1)])
                                sq_free[par] = [("pe", self.c("pe"))]
                            yield
                        if inline_stats:
                            rows_done = self.ln_rows(ps3s, ps3q, th, self.rows["r"], self.rows["mr"])

                    if not inline_stats:
                        with ExitStack() as p3ctx:
                            ps3s = p3ctx.enter_context(nc.psum_tensor(self.u("ps3s"), [1, th], F32))
                            ps3q = p3ctx.enter_context(nc.psum_tensor(self.u("ps3q"), [1, th], F32))
                            f_done = self.c("dve")
                            sq_free = [[], []]
                            for oc in range(NCD):
                                par = oc % 2
                                self.emit("act", lambda e, o=oc, p=par, fbf=fbf, sq3=sq3: e.activation(
                                    sq3[p][:], fbf[:, o * th:(o + 1) * th], AF.Square),
                                    waits=[("dve", f_done)] + sq_free[par], incs=[("act", 1)])
                                actv = self.c("act")

                                def mk_stats(e, o=oc, p=par, ps3s=ps3s, ps3q=ps3q, fbf=fbf, sq3=sq3):
                                    for tt in range(NTT):
                                        e.matmul(ps3s[0:1, tt * NT:(tt + 1) * NT], self.onesB[:],
                                                 fbf[:, o * th + tt * NT: o * th + (tt + 1) * NT],
                                                 start=(o == 0), stop=(o == NCD - 1))
                                    for tt in range(NTT):
                                        inst = e.matmul(ps3q[0:1, tt * NT:(tt + 1) * NT], self.onesB[:],
                                                        sq3[p][:, tt * NT:(tt + 1) * NT],
                                                        start=(o == 0), stop=(o == NCD - 1))
                                    return inst
                                gw = self.guard() if oc == 0 else []
                                self.emit("pe", mk_stats, waits=[("act", actv)] + gw,
                                          incs=[("pe", 1)])
                                sq_free[par] = [("pe", self.c("pe"))]
                                yield
                            rows_done = self.ln_rows(ps3s, ps3q, th, self.rows["r"], self.rows["mr"])

                    with ExitStack() as p4ctx:
                        psR = p4ctx.enter_context(nc.psum_tensor(self.u("psRf"), [128, th], F32))
                        psMR = p4ctx.enter_context(nc.psum_tensor(self.u("psMRf"), [128, th], F32))
                        bc_done = self.bcast(psR, psMR, self.rows["r"], self.rows["mr"], 0, NTT,
                                             waits=[("dve", rows_done)])
                        for ic in range(NCD):
                            par = ic % 2
                            self.emit("sy", lambda e, i=ic, p=par, t0=t0, y1t=y1t: e.dma_start(
                                y1t[p][:], self.y1_sc[i, :, t0:t0 + th]),
                                waits=y1_free[par], incs=[(f"ld_y{par}", 16)])
                            ldv = self.c(f"ld_y{par}")
                            self.emit("dve", lambda e, i=ic, p=par, psR=psR, fbf=fbf, outb=outb: e.tensor_mul(
                                outb[p][:], fbf[:, i * th:(i + 1) * th], psR[:]),
                                waits=[("pe", bc_done)] + out_free[par], incs=[("dve", 1)])
                            self.emit("dve", lambda e, p=par, psMR=psMR, outb=outb: e.tensor_sub(
                                outb[p][:], outb[p][:], psMR[:]),
                                waits=[("dve", self.c("dve"))], incs=[("dve", 1)])
                            self.emit("dve", lambda e, p=par, outb=outb, y1t=y1t: e.tensor_add(
                                outb[p][:], outb[p][:], y1t[p][:]),
                                waits=[("dve", self.c("dve")), (f"ld_y{par}", ldv)],
                                incs=[("dve", 1)])
                            dvv = self.c("dve")
                            y1_free[par] = [("dve", dvv)]
                            self.emit("sy", lambda e, i=ic, p=par, t0=t0, outb=outb: e.dma_start(
                                self.out[i * 128:(i + 1) * 128, t0:t0 + th], outb[p][:]),
                                waits=[("dve", dvv)], incs=[(f"st_o{par}", 16)])
                            out_free[par] = [(f"st_o{par}", self.c(f"st_o{par}"))]
                            yield

    def debug_p0(self):
        nc = self.nc
        dQ = nc.dram_tensor("dQ", [128, NCD * self.T], BF16, kind="ExternalOutput").ap()
        dK = nc.dram_tensor("dK", [NCD, 128, self.S], BF16, kind="ExternalOutput").ap()
        dV = nc.dram_tensor("dV", [NH, self.S, DH], BF16, kind="ExternalOutput").ap()
        self.emit("sy", lambda e: e.dma_start(dQ, self.qbuf[:]), incs=[("st_o0", 16)])
        for oc in range(NCD):
            self.emit("sy", lambda e, o=oc: e.dma_start(dK[o], self.K_sc[o]), incs=[("st_o0", 16)])
        for h in range(NH):
            self.emit("sy", lambda e, h=h: e.dma_start(dV[h], self.V_sc[h]), incs=[("st_o0", 16)])

    def debug_p1(self):
        nc = self.nc
        dO = nc.dram_tensor("dO", [128, NCD * self.T], BF16, kind="ExternalOutput").ap()
        dRS = nc.dram_tensor("dRS", [16, self.T], F32, kind="ExternalOutput").ap()
        self.emit("sy", lambda e: e.dma_start(dO, self.obuf[:]), incs=[("st_o0", 16)])
        self.emit("sy", lambda e: e.dma_start(dRS, self.RS_sc), incs=[("st_o0", 16)])

    def debug_p2(self):
        nc = self.nc
        dY1 = nc.dram_tensor("dY1", [NCD, 128, self.T], F32, kind="ExternalOutput").ap()
        dR2 = nc.dram_tensor("dR2", [1, 512], F32, kind="ExternalOutput").ap()
        dMR2 = nc.dram_tensor("dMR2", [1, 512], F32, kind="ExternalOutput").ap()
        for ic in range(NCD):
            self.emit("sy", lambda e, i=ic: e.dma_start(dY1[i], self.y1_sc[i]), incs=[("st_o0", 16)])
        self.emit("sy", lambda e: e.dma_start(dR2, self.r2row[0:1, 0:512]), incs=[("st_o0", 16)])
        self.emit("sy", lambda e: e.dma_start(dMR2, self.mr2row[0:1, 0:512]), incs=[("st_o0", 16)])

    def finish(self):
        waits = [(s, self.cnt[s]) for s in self.cnt if self.cnt[s] > 0 and s != "bar"]
        self.emit("sy", None, waits=waits)

    def realize(self):
        with self.nc.Block() as blk:
            @blk.tensor
            def _(t):
                for fn in self.streams["pe"]:
                    fn(t)

            @blk.scalar
            def _(s):
                for fn in self.streams["act"]:
                    fn(s)

            @blk.vector
            def _(v):
                for fn in self.streams["dve"]:
                    fn(v)

            @blk.sync
            def _(sy):
                for fn in self.streams["sy"]:
                    fn(sy)

            @blk.gpsimd
            def _(g):
                for fn in self.streams["gp"]:
                    fn(g)
        self.ctx.close()
        return self.nc


def build(S=8192, stage="full"):
    b = Builder(S, stage)
    b.build()
    return b.realize()


def prep_weights(inputs):
    def blk_proj(W):
        Wt = np.ascontiguousarray(np.asarray(W, np.float32).T)
        return np.ascontiguousarray(
            Wt.reshape(NCD, 128, D).transpose(1, 0, 2).reshape(128, NCD * D))

    def blk_ffn(W, nic, noc):
        Wt = np.ascontiguousarray(np.asarray(W, np.float32).T)
        return np.ascontiguousarray(
            Wt.reshape(nic, 128, noc, 128).transpose(2, 1, 0, 3).reshape(noc, 128, nic * 128))

    E = np.zeros((16, D), np.float32)
    for h in range(NH):
        E[h, h * DH:(h + 1) * DH] = 1.0
    f32 = lambda a: np.asarray(a, np.float32)
    return {
        "wq": blk_proj(inputs["L0_attn_Wq"]),
        "wk": blk_proj(inputs["L0_attn_Wk"]),
        "wv": blk_proj(inputs["L0_attn_Wv"]),
        "wo": blk_proj(inputs["L0_attn_Wo"]),
        "w1": blk_ffn(inputs["L0_ffn_w1_w"], NCD, NCH),
        "w2": blk_ffn(inputs["L0_ffn_w2_w"], NCD, NCH),
        "h1": blk_ffn(inputs["L0_ffn_h1_w"], NCH, NCH),
        "h2": blk_ffn(inputs["L0_ffn_h2_w"], NCH, NCH),
        "w3": blk_ffn(inputs["L0_ffn_w3_w"], NCH, NCD),
        "s1": np.ascontiguousarray(f32(inputs["L0_ffn_w1_s"]).reshape(NCH, 128).T),
        "s2": np.ascontiguousarray(f32(inputs["L0_ffn_w2_s"]).reshape(NCH, 128).T),
        "hs1": np.ascontiguousarray(f32(inputs["L0_ffn_h1_s"]).reshape(NCH, 128).T),
        "hs2": np.ascontiguousarray(f32(inputs["L0_ffn_h2_s"]).reshape(NCH, 128).T),
        "s3": np.ascontiguousarray(f32(inputs["L0_ffn_w3_s"]).reshape(NCD, 128).T),
        "Esel": E,
    }


def make_in_maps(inputs, S):
    T = S // 4
    w = prep_weights(inputs)
    x = np.asarray(inputs["L0_attn_x"], np.float32)
    maps = []
    for c in range(8):
        b, q = c // 4, c % 4
        xT = x[b].T
        rot = np.concatenate([xT[:, q * T:(q + 1) * T], xT[:, :q * T], xT[:, (q + 1) * T:]], axis=1)
        m = {"xT": np.ascontiguousarray(rot)}
        m.update(w)
        maps.append(m)
    return maps


def kernel(**inputs):
    x = np.asarray(inputs["L0_attn_x"], np.float32)
    B, S, _ = x.shape
    T = S // 4
    nc = build(S=S, stage="full")
    maps = make_in_maps(inputs, S)
    res = run_bass_kernel_spmd(nc, maps, core_ids=list(range(8)))
    out = np.empty((B, S, D), np.float32)
    for c in range(8):
        b, q = c // 4, c % 4
        out[b, q * T:(q + 1) * T, :] = np.asarray(res.results[c]["out"], np.float32).T
    return out


# version 23 (final)
# speedup vs baseline: 1.1657x; 1.1657x over previous
import numpy as np
from contextlib import ExitStack
import concourse.bass as bass
import concourse.mybir as mybir
from concourse.bass_utils import run_bass_kernel_spmd

F32 = mybir.dt.float32
BF16 = mybir.dt.bfloat16
AF = mybir.ActivationFunctionType
ALU = mybir.AluOpType

D, NH, DH, NCD, DHID, NCH = 1024, 16, 64, 8, 4096, 32
NT = 512
EPS_LN = 1e-5
EPS_ATT = 1e-6
ATT_SCALE = DH ** -0.5

ENGS = ("pe", "act", "dve", "sy", "gp")


class Builder:
    def __init__(self, S, stage="full"):
        self.S = S
        self.T = S // 4
        self.TSUP = min(2048, S)
        self.QM = min(1024, self.T)
        self.TH = min(1024, self.T)
        self.stage = stage
        self.nc = bass.Bass("TRN2", target_bir_lowering=False)
        self.ctx = ExitStack()
        self.streams = {e: [] for e in ENGS}
        self.sems = {}
        self.cnt = {}
        self.bar_n = 0
        self._uid = 0

    def sem(self, name):
        h = self.ctx.enter_context(self.nc.semaphore(name))
        self.sems[name] = h
        self.cnt[name] = 0
        return name

    def emit(self, eng, fn, waits=(), incs=()):
        for s, n in incs:
            self.cnt[s] += n
        wl = [(self.sems[s], v) for s, v in waits]
        il = [(self.sems[s], n) for s, n in incs]

        def closure(e, fn=fn, wl=wl, il=il):
            for h, v in wl:
                e.wait_ge(h, v)
            inst = fn(e) if fn is not None else None
            for h, n in il:
                assert inst is not None
                inst.then_inc(h, n)
        self.streams[eng].append(closure)

    def c(self, s):
        return self.cnt[s]

    def u(self, base):
        self._uid += 1
        return f"{base}_{self._uid}"

    def warmup(self, n=24, extra_waits=()):
        with self.nc.psum_tensor(self.u("warm"), [128, NT], F32) as wp:
            def mk(e, wp=wp, n=n):
                for _ in range(n):
                    inst = e.matmul(wp[:], self.warmsrc[:, 0:128],
                                    self.warmsrc[:, 128:640], start=True, stop=True)
                return inst
            self.emit("pe", mk, waits=[("gp", 1)] + list(extra_waits) + self.guard(),
                      incs=[("pe", 1)])

    def guard(self):
        g = []
        if self.cnt["dve"]:
            g.append(("dve", self.cnt["dve"]))
        if self.cnt["act"]:
            g.append(("act", self.cnt["act"]))
        return g

    def barrier(self):
        self.bar_n += 1
        k = self.bar_n
        waits = [(s, self.cnt[s]) for s in self.cnt if s != "bar" and self.cnt[s] > 0]
        self.emit("sy", lambda e: e.sem_inc(self.sems["bar"], 1), waits=waits)
        self.cnt["bar"] += 1
        for eng in ("pe", "act", "dve", "gp"):
            self.emit(eng, None, waits=[("bar", k)])

    def build(self):
        nc, S, T = self.nc, self.S, self.T
        ctx = self.ctx

        self.xT = nc.dram_tensor("xT", [D, S], F32, kind="ExternalInput").ap()
        self.wq = nc.dram_tensor("wq", [128, NCD * D], F32, kind="ExternalInput").ap()
        self.wk = nc.dram_tensor("wk", [128, NCD * D], F32, kind="ExternalInput").ap()
        self.wv = nc.dram_tensor("wv", [128, NCD * D], F32, kind="ExternalInput").ap()
        self.wo = nc.dram_tensor("wo", [128, NCD * D], F32, kind="ExternalInput").ap()
        self.w1 = nc.dram_tensor("w1", [NCH, 128, NCD * 128], F32, kind="ExternalInput").ap()
        self.w2 = nc.dram_tensor("w2", [NCH, 128, NCD * 128], F32, kind="ExternalInput").ap()
        self.h1 = nc.dram_tensor("h1", [NCH, 128, NCH * 128], F32, kind="ExternalInput").ap()
        self.h2 = nc.dram_tensor("h2", [NCH, 128, NCH * 128], F32, kind="ExternalInput").ap()
        self.w3 = nc.dram_tensor("w3", [NCD, 128, NCH * 128], F32, kind="ExternalInput").ap()
        self.s1 = nc.dram_tensor("s1", [128, NCH], F32, kind="ExternalInput").ap()
        self.s2 = nc.dram_tensor("s2", [128, NCH], F32, kind="ExternalInput").ap()
        self.hs1 = nc.dram_tensor("hs1", [128, NCH], F32, kind="ExternalInput").ap()
        self.hs2 = nc.dram_tensor("hs2", [128, NCH], F32, kind="ExternalInput").ap()
        self.s3 = nc.dram_tensor("s3", [128, NCD], F32, kind="ExternalInput").ap()
        self.Esel = nc.dram_tensor("Esel", [16, D], F32, kind="ExternalInput").ap()
        self.out = nc.dram_tensor("out", [D, T], F32, kind="ExternalOutput").ap()

        self.K_sc = nc.dram_tensor("K_sc", [NCD, 128, S], BF16).ap()
        self.V_sc = nc.dram_tensor("V_sc", [NH, 128, S // 128, DH + 1], BF16).ap()
        self.y1_sc = nc.dram_tensor("y1_sc", [NCD, 128, T], F32).ap()
        self.RS_sc = nc.dram_tensor("RS_sc", [16, T], F32).ap()

        for s in ["bar", "pe", "act", "dve", "gp",
                  "ld_x0", "ld_x1",
                  "ld_w0", "ld_w1", "ld_w2",
                  "ld_k0", "ld_k1",
                  "ld_k0p0", "ld_k0p1", "ld_k0p2", "ld_k0p3",
                  "ld_k1p0", "ld_k1p1", "ld_k1p2", "ld_k1p3",
                  "ld_v0", "ld_v1", "ld_v2", "ld_v3",
                  "ld_y0", "ld_y1",
                  "ld_c",
                  "st_k0", "st_k1",
                  "st_v0", "st_v1",
                  "st_y0", "st_y1",
                  "st_rs", "st_r0", "st_r1",
                  "st_o0", "st_o1"]:
            self.sem(s)

        self.onesF = ctx.enter_context(nc.sbuf_tensor("onesF", [128, 1], F32))
        self.onesB = ctx.enter_context(nc.sbuf_tensor("onesB", [128, 1], BF16))
        self.onesR = ctx.enter_context(nc.sbuf_tensor("onesR", [1, 128], F32))
        self.s1b = ctx.enter_context(nc.sbuf_tensor("s1b", [128, NCH], F32))
        self.s2b = ctx.enter_context(nc.sbuf_tensor("s2b", [128, NCH], F32))
        self.hs1b = ctx.enter_context(nc.sbuf_tensor("hs1b", [128, NCH], F32))
        self.hs2b = ctx.enter_context(nc.sbuf_tensor("hs2b", [128, NCH], F32))
        self.s3b = ctx.enter_context(nc.sbuf_tensor("s3b", [128, NCD], F32))

        self.warmsrc = ctx.enter_context(nc.sbuf_tensor("warmsrc", [128, 640], BF16))
        self.emit("gp", lambda e: e.memset(self.warmsrc[:], 1.0), incs=[("gp", 1)])
        self.emit("gp", lambda e: e.memset(self.onesF[:], 1.0), incs=[("gp", 1)])
        self.emit("gp", lambda e: e.memset(self.onesB[:], 1.0), incs=[("gp", 1)])
        self.emit("gp", lambda e: e.memset(self.onesR[:], 1.0), incs=[("gp", 1)])
        for src, dst in [(self.s1, self.s1b), (self.s2, self.s2b),
                         (self.hs1, self.hs1b), (self.hs2, self.hs2b), (self.s3, self.s3b)]:
            self.emit("sy", lambda e, s=src, d=dst: e.dma_start(d[:], s), incs=[("ld_c", 16)])

        scope_q = ExitStack()
        scope_o = ExitStack()
        self.qbuf = scope_q.enter_context(nc.sbuf_tensor("qbuf", [128, NCD * T], BF16))
        self.phase0()
        self.barrier()
        if self.stage == "p0":
            self.debug_p0()
            self.finish()
            scope_q.close()
            scope_o.close()
            return
        self.obuf = scope_o.enter_context(nc.sbuf_tensor("obuf", [128, NCD * T], BF16, side="right"))
        scope_p1 = ExitStack()
        self.open_p1(scope_p1)
        TW = 512 if S <= 4096 else min(1024, T)
        qms_a = list(range(TW // 512))
        qms_b = list(range(TW // 512, T // 512))
        def p1_psums(pctx, kct, nkb=2):
            psS = [pctx.enter_context(nc.psum_tensor(self.u(f"psS{i}"), [128, kct * 512], F32))
                   for i in range(2)]
            psO = [pctx.enter_context(nc.psum_tensor(self.u(f"psO{i}"), [DH + 1, 512], F32))
                   for i in range(2)]
            self.wtil = [pctx.enter_context(nc.sbuf_tensor(self.u(f"wtil{i}"), [128, kct * 512], BF16))
                         for i in range(4)]
            self.wtil_free = [self.guard() for _ in range(4)]
            self.kbuf = [pctx.enter_context(nc.sbuf_tensor(self.u(f"kbuf{i}"), [128, S], BF16))
                         for i in range(nkb)]
            self.kbuf_free = [self.guard() for _ in range(nkb)]
            return psS, psO

        if self.stage == "p1":
            with ExitStack() as pctx:
                psS, psO = p1_psums(pctx, 3)
                for _ in self.gen_phase1(qms_a + qms_b, 3, psS, psO):
                    pass
            self.barrier()
            self.debug_p1()
            self.finish()
            scope_p1.close()
            scope_q.close()
            scope_o.close()
            return
        with ExitStack() as pctx:
            psS, psO = p1_psums(pctx, 3)
            for _ in self.gen_phase1(qms_a, 3, psS, psO):
                pass
        self.barrier()
        scope_lnA = ExitStack()
        ln2tA = scope_lnA.enter_context(nc.sbuf_tensor("ln2tA", [128, NCD * TW], BF16))
        self.phase2(0, TW, ln2tA)
        self.barrier()
        if self.stage == "p2":
            self.debug_p2()
            self.finish()
            scope_lnA.close()
            scope_p1.close()
            scope_q.close()
            scope_o.close()
            return
        def gen_ffn_a():
            for t0f in range(0, TW, 512):
                for _ in self.gen_phase3(t0f, 512, ln2tA, 0, TW):
                    yield
        if qms_b:
            with ExitStack() as pctx:
                psS, psO = p1_psums(pctx, 2, nkb=1)
                g_mid = self.gen_phase1(qms_b, 2, psS, psO)
                g_ffn = gen_ffn_a()
                mid_alive = ffn_alive = True
                while mid_alive or ffn_alive:
                    if mid_alive:
                        try:
                            next(g_mid)
                        except StopIteration:
                            mid_alive = False
                    if ffn_alive:
                        try:
                            next(g_ffn)
                        except StopIteration:
                            ffn_alive = False
        else:
            for _ in self.gen_phase3(0, 512 if T >= 512 else T, ln2tA, 0, TW,
                                     inline_stats=True):
                pass
        self.barrier()
        scope_lnA.close()
        scope_p1.close()
        scope_q.close()
        if qms_b:
            scope_lnB = ExitStack()
            ln2tB = scope_lnB.enter_context(nc.sbuf_tensor("ln2tB", [128, NCD * TW], BF16))
            self.phase2(TW, TW, ln2tB)
            self.barrier()
            scope_o.close()
            for t0f in range(TW, T, 1024):
                for _ in self.gen_phase3(t0f, min(1024, T - t0f), ln2tB, TW, TW,
                                         inline_stats=True):
                    pass
            scope_lnB.close()
        else:
            scope_o.close()
        self.finish()

    def phase0(self):
        nc, S, T, TSUP = self.nc, self.S, self.T, self.TSUP
        NSUP = S // TSUP
        NTT = TSUP // NT

        with ExitStack() as sc:
            wqb = sc.enter_context(nc.sbuf_tensor("wqb", [128, NCD * D], BF16))
            wkb = sc.enter_context(nc.sbuf_tensor("wkb", [128, NCD * D], BF16))
            wvb = sc.enter_context(nc.sbuf_tensor("wvb", [128, NCD * D], BF16))
            xs = [sc.enter_context(nc.sbuf_tensor(f"xs{i}", [128, TSUP], F32)) for i in range(2)]
            sq = [sc.enter_context(nc.sbuf_tensor(f"sq{i}", [128, TSUP], BF16)) for i in range(2)]
            xh = [sc.enter_context(nc.sbuf_tensor(f"xh{i}", [128, TSUP], BF16)) for i in range(2)]
            lnb = sc.enter_context(nc.sbuf_tensor("lnb", [128, NCD * TSUP], BF16))
            kvo = [sc.enter_context(nc.sbuf_tensor(f"kvo{i}", [128, TSUP], BF16)) for i in range(2)]
            vvo = [sc.enter_context(nc.sbuf_tensor(f"vvo{i}", [128, NH * (DH + 1)], BF16))
                   for i in range(2)]
            self.alloc_rows(sc, TSUP, "p0")
            for b in range(2):
                for hh in range(NH):
                    self.emit("gp", lambda e, b=b, hh=hh: e.memset(
                        vvo[b][:, hh * (DH + 1) + DH: hh * (DH + 1) + DH + 1], 1.0),
                        incs=[("gp", 1)])

            self.warmup()
            self.emit("gp", lambda e: e.dma_start(wqb[:], self.wq), incs=[("ld_w0", 16)])
            self.emit("gp", lambda e: e.dma_start(wkb[:], self.wk), incs=[("ld_w0", 16)])
            self.emit("gp", lambda e: e.dma_start(wvb[:], self.wv), incs=[("ld_w0", 16)])
            wproj_ready = self.c("ld_w0")

            xs_free = [[], []]
            sq_free = [[], []]
            kvo_free = [[], []]
            vvo_free = [[], []]

            for sup in range(NSUP):
                ts0 = sup * TSUP
                with ExitStack() as pctx:
                    ps_sum = pctx.enter_context(nc.psum_tensor(self.u("ps_sum"), [1, TSUP], F32))
                    ps_ssq = pctx.enter_context(nc.psum_tensor(self.u("ps_ssq"), [1, TSUP], F32))
                    for ic in range(NCD):
                        par = ic % 2
                        self.emit("sy",
                                  lambda e, p=par, i=ic, ts0=ts0: e.dma_start(
                                      xs[p][:], self.xT[i * 128:(i + 1) * 128, ts0:ts0 + TSUP]),
                                  waits=xs_free[par], incs=[(f"ld_x{par}", 16)])
                        ldv = self.c(f"ld_x{par}")
                        self.emit("act",
                                  lambda e, p=par: e.activation(sq[p][:], xs[p][:], AF.Square),
                                  waits=[(f"ld_x{par}", ldv)] + sq_free[par], incs=[("act", 1)])
                        self.emit("act",
                                  lambda e, p=par: e.activation(xh[p][:], xs[p][:], AF.Copy),
                                  waits=[(f"ld_x{par}", ldv)] + sq_free[par], incs=[("act", 1)])
                        actv = self.c("act")

                        def mk_stats(e, p=par, i=ic, ps_sum=ps_sum, ps_ssq=ps_ssq):
                            for tt in range(NTT):
                                e.matmul(ps_sum[0:1, tt * NT:(tt + 1) * NT], self.onesB[:],
                                         xh[p][:, tt * NT:(tt + 1) * NT],
                                         start=(i == 0), stop=(i == NCD - 1))
                            for tt in range(NTT):
                                inst = e.matmul(ps_ssq[0:1, tt * NT:(tt + 1) * NT], self.onesB[:],
                                                sq[p][:, tt * NT:(tt + 1) * NT],
                                                start=(i == 0), stop=(i == NCD - 1))
                            return inst
                        gw = self.guard() if ic == 0 else []
                        self.emit("pe", mk_stats,
                                  waits=[(f"ld_x{par}", ldv), ("act", actv), ("gp", 4)] + gw,
                                  incs=[("pe", 1)])
                        xs_free[par] = [("pe", self.c("pe"))]
                        sq_free[par] = [("pe", self.c("pe"))]
                    rows_done = self.ln_rows(ps_sum, ps_ssq, TSUP, self.rows["r"], self.rows["mr"])

                with ExitStack() as pctx:
                    psR = pctx.enter_context(nc.psum_tensor(self.u("psR"), [128, TSUP], F32))
                    psMR = pctx.enter_context(nc.psum_tensor(self.u("psMR"), [128, TSUP], F32))
                    bc_done = self.bcast(psR, psMR, self.rows["r"], self.rows["mr"], 0, NTT,
                                         waits=[("dve", rows_done)])
                    for ic in range(NCD):
                        par = ic % 2
                        self.emit("sy",
                                  lambda e, p=par, i=ic, ts0=ts0: e.dma_start(
                                      xs[p][:], self.xT[i * 128:(i + 1) * 128, ts0:ts0 + TSUP]),
                                  waits=xs_free[par], incs=[(f"ld_x{par}", 16)])
                        ldv = self.c(f"ld_x{par}")
                        self.emit("dve", lambda e, p=par, psR=psR: e.tensor_mul(
                            xs[p][:], xs[p][:], psR[:]),
                            waits=[(f"ld_x{par}", ldv), ("pe", bc_done)], incs=[("dve", 1)])
                        self.emit("dve", lambda e, p=par, i=ic, psMR=psMR: e.tensor_sub(
                            lnb[:, i * TSUP:(i + 1) * TSUP], xs[p][:], psMR[:]),
                            waits=[("dve", self.c("dve"))], incs=[("dve", 1)])
                        xs_free[par] = [("dve", self.c("dve"))]
                    ln_ready = self.c("dve")

                with ExitStack() as pctx:
                    psP = [pctx.enter_context(nc.psum_tensor(self.u(f"psP{i}"), [128, TSUP], F32))
                           for i in range(2)]
                    psP_free = [[], []]
                    for oc in range(NCD):
                        par = oc % 2

                        def mk_kproj(e, o=oc, p=par, psP=psP):
                            for ic in range(NCD):
                                for tt in range(NTT):
                                    inst = e.matmul(
                                        psP[p][:, tt * NT:(tt + 1) * NT],
                                        wkb[:, ic * D + o * 128: ic * D + o * 128 + 128],
                                        lnb[:, ic * TSUP + tt * NT: ic * TSUP + (tt + 1) * NT],
                                        start=(ic == 0), stop=(ic == NCD - 1))
                            return inst
                        self.emit("pe", mk_kproj,
                                  waits=[("dve", ln_ready), ("ld_w0", wproj_ready)] + psP_free[par],
                                  incs=[("pe", 1)])
                        mmv = self.c("pe")
                        self.emit("dve", lambda e, p=par, psP=psP: e.tensor_copy(kvo[p][:], psP[p][:]),
                                  waits=[("pe", mmv)] + kvo_free[par], incs=[("dve", 1)])
                        psP_free[par] = [("dve", self.c("dve"))]
                        dvv = self.c("dve")
                        self.emit("gp", lambda e, p=par, o=oc, ts0=ts0: e.dma_start(
                            self.K_sc[o, :, ts0:ts0 + TSUP], kvo[p][:]),
                            waits=[("dve", dvv)], incs=[(f"st_k{par}", 16)])
                        kvo_free[par] = [(f"st_k{par}", self.c(f"st_k{par}"))]
                    if sup == 0:
                        NTTQ = T // NT
                        for oc in range(NCD):
                            par = oc % 2

                            def mk_qproj(e, o=oc, p=par, psP=psP):
                                for ic in range(NCD):
                                    for tt in range(NTTQ):
                                        inst = e.matmul(
                                            psP[p][:, tt * NT:(tt + 1) * NT],
                                            wqb[:, ic * D + o * 128: ic * D + o * 128 + 128],
                                            lnb[:, ic * TSUP + tt * NT: ic * TSUP + (tt + 1) * NT],
                                            start=(ic == 0), stop=(ic == NCD - 1))
                                return inst
                            self.emit("pe", mk_qproj,
                                      waits=[("dve", ln_ready), ("ld_w0", wproj_ready)] + psP_free[par],
                                      incs=[("pe", 1)])
                            mmv = self.c("pe")
                            self.emit("dve", lambda e, p=par, o=oc, psP=psP: e.tensor_copy(
                                self.qbuf[:, o * T:(o + 1) * T], psP[p][:, 0:T]),
                                waits=[("pe", mmv)], incs=[("dve", 1)])
                            psP_free[par] = [("dve", self.c("dve"))]

                with ExitStack() as pctx:
                    psV = [pctx.enter_context(nc.psum_tensor(self.u(f"psV{i}"), [128, D], F32))
                           for i in range(2)]
                    psV_free = [self.guard(), self.guard()]
                    for tk in range(TSUP // 128):
                        par = tk % 2

                        def mk_vproj(e, t=tk, p=par, psV=psV):
                            for ic in range(NCD):
                                for oh in range(2):
                                    inst = e.matmul(
                                        psV[p][:, oh * NT:(oh + 1) * NT],
                                        lnb[:, ic * TSUP + t * 128: ic * TSUP + t * 128 + 128],
                                        wvb[:, ic * D + oh * NT: ic * D + (oh + 1) * NT],
                                        start=(ic == 0), stop=(ic == NCD - 1))
                            return inst
                        self.emit("pe", mk_vproj,
                                  waits=[("dve", ln_ready), ("ld_w0", wproj_ready)] + psV_free[par],
                                  incs=[("pe", 1)])
                        mmv = self.c("pe")
                        def mk_vevac(e, p=par, psV=psV):
                            dst = bass.AP(vvo[p], 0, [[NH * (DH + 1), 128], [DH + 1, NH], [1, DH]])
                            return e.tensor_copy(dst, psV[p][:])
                        self.emit("dve", mk_vevac,
                                  waits=[("pe", mmv), ("gp", self.c("gp"))] + vvo_free[par],
                                  incs=[("dve", 1)])
                        psV_free[par] = [("dve", self.c("dve"))]
                        dvv = self.c("dve")
                        tkg = (ts0 + tk * 128) // 128
                        NKC_ALL = self.S // 128

                        def mk_vst(e, p=par, tkg=tkg, NKC_ALL=NKC_ALL):
                            dst = bass.AP(self.V_sc.tensor, tkg * (DH + 1),
                                          [[NKC_ALL * (DH + 1), 128],
                                           [128 * NKC_ALL * (DH + 1), NH], [1, DH + 1]])
                            src = bass.AP(vvo[p], 0,
                                          [[NH * (DH + 1), 128], [DH + 1, NH], [1, DH + 1]])
                            return e.dma_start(dst, src)
                        self.emit("gp", mk_vst, waits=[("dve", dvv)], incs=[(f"st_v{par}", 16)])
                        vvo_free[par] = [(f"st_v{par}", self.c(f"st_v{par}"))]

    def alloc_rows(self, scope, W, tag):
        nc = self.nc
        r = {}
        for n in ("m", "v", "r", "mr"):
            r[n] = scope.enter_context(nc.sbuf_tensor(f"row_{tag}_{n}", [1, W], F32))
        self.rows = r

    def ln_rows(self, ps_sum, ps_ssq, W, rdst, mrdst, roff=0, rdst_free=()):
        pev = self.c("pe")
        ro = self.rows
        self.emit("dve", lambda e, ro=ro: e.tensor_scalar_mul(
            ro["m"][:, 0:W], ps_sum[:], 1.0 / D),
            waits=[("pe", pev)], incs=[("dve", 1)])
        self.emit("dve", lambda e, ro=ro: e.tensor_scalar(
            ro["v"][:, 0:W], ps_ssq[:], 1.0 / D, EPS_LN, ALU.mult, ALU.add),
            waits=[("pe", pev)], incs=[("dve", 1)])
        d0 = self.c("dve")
        self.emit("dve", lambda e, ro=ro: e.tensor_mul(
            rdst[:, roff:roff + W], ro["m"][:, 0:W], ro["m"][:, 0:W]),
            waits=[("dve", d0)] + list(rdst_free), incs=[("dve", 1)])
        self.emit("dve", lambda e, ro=ro: e.tensor_sub(
            ro["v"][:, 0:W], ro["v"][:, 0:W], rdst[:, roff:roff + W]),
            waits=[("dve", self.c("dve"))], incs=[("dve", 1)])
        dv = self.c("dve")
        self.emit("act", lambda e, ro=ro: e.activation(
            ro["v"][:, 0:W], ro["v"][:, 0:W], AF.Ln),
            waits=[("dve", dv)], incs=[("act", 1)])
        self.emit("act", lambda e, ro=ro: e.activation(
            rdst[:, roff:roff + W], ro["v"][:, 0:W], AF.Exp, scale=-0.5),
            waits=[("act", self.c("act"))], incs=[("act", 1)])
        av = self.c("act")
        self.emit("dve", lambda e, ro=ro: e.tensor_mul(
            mrdst[:, roff:roff + W], ro["m"][:, 0:W], rdst[:, roff:roff + W]),
            waits=[("act", av)], incs=[("dve", 1)])
        return self.c("dve")

    def bcast(self, psR, psMR, rsrc, mrsrc, roff, ntt, waits):
        def mk(e):
            for tt in range(ntt):
                e.matmul(psR[:, tt * NT:(tt + 1) * NT], self.onesR[:],
                         rsrc[0:1, roff + tt * NT: roff + (tt + 1) * NT], start=True, stop=True)
            for tt in range(ntt):
                inst = e.matmul(psMR[:, tt * NT:(tt + 1) * NT], self.onesR[:],
                                mrsrc[0:1, roff + tt * NT: roff + (tt + 1) * NT], start=True, stop=True)
            return inst
        self.emit("pe", mk, waits=list(waits) + self.guard(), incs=[("pe", 1)])
        return self.c("pe")

    def open_p1(self, sc):
        nc, S = self.nc, self.S
        self.vbuf = [sc.enter_context(nc.sbuf_tensor(f"vbuf{i}", [128, 8 * (DH + 1)], BF16))
                     for i in range(4)]
        self.rs_row = [sc.enter_context(nc.sbuf_tensor(f"rs_row{i}", [1, 512], F32))
                       for i in range(2)]
        self.sig_pre = sc.enter_context(nc.sbuf_tensor("sig_pre", [1, NT], BF16))
        self.rs_free = [[], []]
        self.vbuf_free = [[], [], [], []]

    def gen_phase1(self, qms, kct, psS, psO):
        nc, S, T = self.nc, self.S, self.T
        QM = 512
        NKC = S // 128
        tiles = []
        k = 0
        while k < NKC:
            n = min(kct, NKC - k)
            tiles.append((k, n))
            k += n
        NKC_ALL = NKC
        kbuf, vbuf, wtil, rs_row = self.kbuf, self.vbuf, self.wtil, self.rs_row
        kbuf_free, rs_free = self.kbuf_free, self.rs_free
        vbuf_free, wtil_free = self.vbuf_free, self.wtil_free

        if True:
            psS_free = [self.guard(), self.guard()]
            psO_free = [self.guard(), self.guard()]

            self.emit("act", lambda e: e.activation(
                self.sig_pre[:], self.warmsrc[0:1, 0:NT], AF.Sigmoid),
                waits=[("gp", 1), ("bar", self.bar_n)], incs=[("act", 1)])
            first = [True]
            for qm in qms:
                q0 = qm * QM
                for hp in range(NCD):
                    kpar = hp % len(kbuf)
                    SP4 = S // 4
                    k_parts = []
                    self.emit("sy", lambda e, h=hp, p=kpar, SP4=SP4: e.dma_start(
                        kbuf[p][:, 0:SP4], self.K_sc[h, :, 0:SP4]),
                        waits=kbuf_free[kpar], incs=[(f"ld_k{kpar}p0", 16)])
                    k_parts.append((f"ld_k{kpar}p0", self.c(f"ld_k{kpar}p0")))

                    def emit_k_rest(hp=hp, kpar=kpar, SP4=SP4, k_parts=k_parts):
                        for pt in range(1, 4):
                            self.emit("sy", lambda e, h=hp, p=kpar, pt=pt, SP4=SP4: e.dma_start(
                                kbuf[p][:, pt * SP4:(pt + 1) * SP4],
                                self.K_sc[h, :, pt * SP4:(pt + 1) * SP4]),
                                waits=kbuf_free[kpar], incs=[(f"ld_k{kpar}p{pt}", 16)])
                            k_parts.append((f"ld_k{kpar}p{pt}", self.c(f"ld_k{kpar}p{pt}")))
                    k_ld = None
                    if first[0]:
                        first[0] = False

                        def mk_warm(e):
                            for _ in range(16):
                                inst = e.matmul(psS[1][:, 0:NT], self.warmsrc[:, 0:128],
                                                self.warmsrc[:, 128:640],
                                                start=True, stop=True)
                            return inst
                        self.emit("pe", mk_warm,
                                  waits=[("gp", 1), k_parts[0]] + psS_free[1],
                                  incs=[("pe", 1)])
                        psS_free[1] = [("pe", self.c("pe"))]
                    pending_pv = None
                    v_lds = {}
                    for ti, (kc0, nk) in enumerate(tiles):
                        if ti == 1:
                            emit_k_rest()
                        for kc in range(kc0, kc0 + nk):
                            if kc % 8 == 0:
                                kb = kc // 8
                                for hh in range(2):
                                    vslot = hh * 2 + kb % 2
                                    self.emit("sy", lambda e, hh_=2 * hp + hh, kb=kb, p=vslot:
                                              e.dma_start(vbuf[p][:], bass.AP(
                                                  self.V_sc.tensor,
                                                  hh_ * 128 * NKC_ALL * (DH + 1) + kb * 8 * (DH + 1),
                                                  [[NKC_ALL * (DH + 1), 128],
                                                   [1, 8 * (DH + 1)]])),
                                              waits=vbuf_free[vslot],
                                              incs=[(f"ld_v{vslot}", 16)])
                                    v_lds[(hh, kb)] = self.c(f"ld_v{vslot}")

                        sig_done = [0, 0]
                        tp_ = ti % 2
                        for hh in range(2):
                            def mk_sc(e, kp=kpar, k0=kc0, n=nk, hp_=hp, q0=q0, hh=hh):
                                b = hh * 64
                                for j in range(n):
                                    inst = e.matmul(
                                        psS[hh][:, j * QM:(j + 1) * QM],
                                        kbuf[kp][b:b + 64, (k0 + j) * 128:(k0 + j + 1) * 128],
                                        self.qbuf[b:b + 64,
                                                  hp_ * T + q0: hp_ * T + q0 + QM],
                                        start=True, stop=True)
                                return inst
                            kpart_needed = min(((kc0 + nk - 1) * 128) // (S // 4),
                                               len(k_parts) - 1)
                            self.emit("pe", mk_sc,
                                      waits=[k_parts[kpart_needed]] + psS_free[hh],
                                      incs=[("pe", 1)])
                            sc_done = self.c("pe")
                            w = hh * 2 + tp_
                            self.emit("act", lambda e, w=w, hh=hh, n=nk: e.activation(
                                wtil[w][:, 0:n * QM], psS[hh][:, 0:n * QM],
                                AF.Sigmoid, scale=ATT_SCALE),
                                waits=[("pe", sc_done)] + wtil_free[w],
                                incs=[("act", 1)])
                            sig_done[hh] = self.c("act")
                            psS_free[hh] = [("act", sig_done[hh])]

                        if pending_pv is not None:
                            pending_pv()

                        vwaits = [[], []]
                        for hh in range(2):
                            for kc in range(kc0, kc0 + nk):
                                kb = kc // 8
                                vwaits[hh].append((f"ld_v{hh * 2 + kb % 2}", v_lds[(hh, kb)]))
                            vwaits[hh] = sorted(set(vwaits[hh]))

                        def mk_pv_emit(k0=kc0, n=nk, sig=tuple(sig_done), tp_=tp_,
                                       vw=(tuple(vwaits[0]), tuple(vwaits[1]))):
                            for hh in range(2):
                                w = hh * 2 + tp_
                                def mk_pv(e, hh=hh, w=w):
                                    for j in range(n):
                                        k = k0 + j
                                        a_ = k % 8
                                        vslot = hh * 2 + (k // 8) % 2
                                        inst = e.matmul(
                                            psO[hh][:, :],
                                            vbuf[vslot][:, a_ * (DH + 1): a_ * (DH + 1) + DH + 1],
                                            wtil[w][:, j * QM:(j + 1) * QM],
                                            start=(k == 0), stop=(k == NKC - 1))
                                    return inst
                                pvw = [("act", sig[hh])] + list(vw[hh])
                                if k0 == 0:
                                    pvw += psO_free[hh]
                                self.emit("pe", mk_pv, waits=pvw, incs=[("pe", 1)])
                                wtil_free[w] = [("pe", self.c("pe"))]
                                for kc in range(k0, k0 + n):
                                    if kc % 8 == 7 or kc == NKC - 1:
                                        vbuf_free[hh * 2 + (kc // 8) % 2] = [("pe", self.c("pe"))]
                        pending_pv = mk_pv_emit
                        yield
                    pending_pv()
                    pending_pv = None
                    pv_all = self.c("pe")
                    for hh in range(2):
                        h = 2 * hp + hh
                        self.emit("dve", lambda e, hh=hh: e.tensor_copy(
                            rs_row[hh][:], psO[hh][64:65, :]),
                            waits=[("pe", pv_all)] + rs_free[hh], incs=[("dve", 1)])
                        rsd = self.c("dve")
                        self.emit("gp", lambda e, hh=hh, h=h, q=q0: e.dma_start(
                            self.RS_sc[h:h + 1, q:q + QM], rs_row[hh][:]),
                            waits=[("dve", rsd)], incs=[(f"st_r{hh}", 16)])
                        rs_free[hh] = [(f"st_r{hh}", self.c(f"st_r{hh}"))]
                        self.emit("dve", lambda e, hh=hh, c=hp, q=q0: e.tensor_copy(
                            self.obuf[hh * 64:hh * 64 + 64, c * T + q: c * T + q + QM],
                            psO[hh][0:64, :]),
                            waits=[("pe", pv_all)], incs=[("dve", 1)])
                        psO_free[hh] = [("dve", self.c("dve"))]
                    yield
                    kbuf_free[kpar] = [("pe", self.c("pe"))]

    def phase2(self, t0, TW, ln2t):
        nc, T = self.nc, self.T
        NTT = TW // NT

        self.warmup()
        with ExitStack() as sc:
            wob = sc.enter_context(nc.sbuf_tensor(self.u("wob"), [128, NCD * D], BF16))
            abuf = sc.enter_context(nc.sbuf_tensor(self.u("abuf"), [128, NCD * TW], BF16, side="right"))
            Eb = sc.enter_context(nc.sbuf_tensor(self.u("Eb"), [16, D], F32))
            RSm = sc.enter_context(nc.sbuf_tensor(self.u("RSm"), [16, TW], F32))
            self.alloc_rows(sc, TW, self.u("p2"))
            r2loc = sc.enter_context(nc.sbuf_tensor(self.u("r2loc"), [1, TW], F32))
            mr2loc = sc.enter_context(nc.sbuf_tensor(self.u("mr2loc"), [1, TW], F32))
            self.r2row, self.mr2row = r2loc, mr2loc
            sq2 = [sc.enter_context(nc.sbuf_tensor(self.u(f"sq2{i}"), [128, TW], BF16)) for i in range(2)]
            xb2 = [sc.enter_context(nc.sbuf_tensor(self.u(f"xb2{i}"), [128, TW], F32)) for i in range(2)]
            y1t = [sc.enter_context(nc.sbuf_tensor(self.u(f"y1t{i}"), [128, TW], F32)) for i in range(2)]

            self.emit("gp", lambda e: e.dma_start(wob[:], self.wo),
                      waits=[("bar", self.bar_n)], incs=[("ld_w1", 16)])
            wo_ready = self.c("ld_w1")

            self.emit("sy", lambda e, RSm=RSm, t0=t0: e.dma_start(
                RSm[:], self.RS_sc[:, t0:t0 + TW]), incs=[("st_rs", 16)])
            self.emit("sy", lambda e, Eb=Eb: e.dma_start(Eb[:], self.Esel),
                      incs=[("st_rs", 16)])
            rsv = self.c("st_rs")
            self.emit("dve", lambda e, RSm=RSm: e.tensor_scalar_add(RSm[:], RSm[:], EPS_ATT),
                      waits=[("st_rs", rsv)], incs=[("dve", 1)])
            self.emit("dve", lambda e, RSm=RSm: e.reciprocal(RSm[:], RSm[:]),
                      waits=[("dve", self.c("dve"))], incs=[("dve", 1)])
            rs_done = self.c("dve")

            with ExitStack() as pctx:
                psB = [pctx.enter_context(nc.psum_tensor(self.u(f"psB{i}"), [128, TW], F32))
                       for i in range(2)]
                psB_free = [self.guard(), self.guard()]
                for c in range(NCD):
                    par = c % 2

                    def mk_b(e, c=c, p=par, Eb=Eb, RSm=RSm, psB=psB):
                        for tt in range(NTT):
                            inst = e.matmul(psB[p][:, tt * NT:(tt + 1) * NT],
                                            Eb[:, c * 128:(c + 1) * 128],
                                            RSm[0:16, tt * NT:(tt + 1) * NT],
                                            start=True, stop=True)
                        return inst
                    self.emit("pe", mk_b, waits=[("dve", rs_done)] + psB_free[par],
                              incs=[("pe", 1)])
                    bv = self.c("pe")
                    self.emit("dve", lambda e, c=c, p=par, psB=psB, t0=t0: e.tensor_mul(
                        self.obuf[:, c * T + t0: c * T + t0 + TW],
                        self.obuf[:, c * T + t0: c * T + t0 + TW], psB[p][:]),
                        waits=[("pe", bv)], incs=[("dve", 1)])
                    psB_free[par] = [("dve", self.c("dve"))]
                onorm_done = self.c("dve")

            with ExitStack() as pctx:
                psA = [pctx.enter_context(nc.psum_tensor(self.u(f"psA{i}"), [128, TW], F32))
                       for i in range(2)]
                psA_free = [[], []]
                for oc in range(NCD):
                    par = oc % 2

                    def mk_wo(e, o=oc, p=par, psA=psA, wob=wob, t0=t0):
                        for ic in range(NCD):
                            for tt in range(NTT):
                                inst = e.matmul(
                                    psA[p][:, tt * NT:(tt + 1) * NT],
                                    wob[:, ic * D + o * 128: ic * D + o * 128 + 128],
                                    self.obuf[:, ic * T + t0 + tt * NT: ic * T + t0 + (tt + 1) * NT],
                                    start=(ic == 0), stop=(ic == NCD - 1))
                        return inst
                    self.emit("pe", mk_wo,
                              waits=[("dve", onorm_done), ("ld_w1", wo_ready)] + psA_free[par],
                              incs=[("pe", 1)])
                    mmv = self.c("pe")
                    self.emit("dve", lambda e, o=oc, p=par, psA=psA, abuf=abuf: e.tensor_copy(
                        abuf[:, o * TW:(o + 1) * TW], psA[p][:]),
                        waits=[("pe", mmv)], incs=[("dve", 1)])
                    psA_free[par] = [("dve", self.c("dve"))]
                a_done = self.c("dve")

            with ExitStack() as pctx:
                ps_sum = pctx.enter_context(nc.psum_tensor(self.u("ps_sum2"), [1, TW], F32))
                ps_ssq = pctx.enter_context(nc.psum_tensor(self.u("ps_ssq2"), [1, TW], F32))
                sq_free = [[], []]
                for ic in range(NCD):
                    par = ic % 2
                    self.emit("act", lambda e, i=ic, p=par, abuf=abuf, sq2=sq2: e.activation(
                        sq2[p][:], abuf[:, i * TW:(i + 1) * TW], AF.Square),
                        waits=[("dve", a_done)] + sq_free[par], incs=[("act", 1)])
                    actv = self.c("act")

                    def mk_stats(e, i=ic, p=par, abuf=abuf, sq2=sq2, ps_sum=ps_sum, ps_ssq=ps_ssq):
                        for tt in range(NTT):
                            e.matmul(ps_sum[0:1, tt * NT:(tt + 1) * NT], self.onesB[:],
                                     abuf[:, i * TW + tt * NT: i * TW + (tt + 1) * NT],
                                     start=(i == 0), stop=(i == NCD - 1))
                        for tt in range(NTT):
                            inst = e.matmul(ps_ssq[0:1, tt * NT:(tt + 1) * NT], self.onesB[:],
                                            sq2[p][:, tt * NT:(tt + 1) * NT],
                                            start=(i == 0), stop=(i == NCD - 1))
                        return inst
                    gw = self.guard() if ic == 0 else []
                    self.emit("pe", mk_stats, waits=[("act", actv), ("dve", a_done)] + gw,
                              incs=[("pe", 1)])
                    sq_free[par] = [("pe", self.c("pe"))]
                rows_done = self.ln_rows(ps_sum, ps_ssq, TW, self.rows["r"], self.rows["mr"])

            xb_free = [[], []]
            y1_free = [[], []]
            with ExitStack() as pctx:
                psR = pctx.enter_context(nc.psum_tensor(self.u("psRy"), [128, TW], F32))
                psMR = pctx.enter_context(nc.psum_tensor(self.u("psMRy"), [128, TW], F32))
                ps2s = pctx.enter_context(nc.psum_tensor(self.u("ps2s"), [1, TW], F32))
                ps2q = pctx.enter_context(nc.psum_tensor(self.u("ps2q"), [1, TW], F32))
                bc_done = self.bcast(psR, psMR, self.rows["r"], self.rows["mr"], 0, NTT,
                                     waits=[("dve", rows_done)])
                for ic in range(NCD):
                    par = ic % 2
                    self.emit("sy", lambda e, i=ic, p=par, t0=t0, xb2=xb2: e.dma_start(
                        xb2[p][:], self.xT[i * 128:(i + 1) * 128, t0:t0 + TW]),
                        waits=xb_free[par], incs=[(f"ld_x{par}", 16)])
                    ldv = self.c(f"ld_x{par}")
                    self.emit("dve", lambda e, i=ic, p=par, psR=psR, abuf=abuf, y1t=y1t: e.tensor_mul(
                        y1t[p][:], abuf[:, i * TW:(i + 1) * TW], psR[:]),
                        waits=[("pe", bc_done)] + y1_free[par], incs=[("dve", 1)])
                    self.emit("dve", lambda e, p=par, psMR=psMR, y1t=y1t: e.tensor_sub(
                        y1t[p][:], y1t[p][:], psMR[:]),
                        waits=[("dve", self.c("dve"))], incs=[("dve", 1)])
                    self.emit("dve", lambda e, p=par, xb2=xb2, y1t=y1t: e.tensor_add(
                        y1t[p][:], y1t[p][:], xb2[p][:]),
                        waits=[("dve", self.c("dve")), (f"ld_x{par}", ldv)],
                        incs=[("dve", 1)])
                    dvv = self.c("dve")
                    xb_free[par] = [("dve", dvv)]
                    self.emit("gp", lambda e, i=ic, p=par, t0=t0, y1t=y1t: e.dma_start(
                        self.y1_sc[i, :, t0:t0 + TW], y1t[p][:]),
                        waits=[("dve", dvv)], incs=[(f"st_y{par}", 16)])
                    self.emit("act", lambda e, p=par, sq2=sq2, y1t=y1t: e.activation(
                        sq2[p][:, 0:TW], y1t[p][:], AF.Square),
                        waits=[("dve", dvv)], incs=[("act", 1)])
                    actv = self.c("act")

                    def mk_stats2(e, i=ic, p=par, ps2s=ps2s, ps2q=ps2q, sq2=sq2, y1t=y1t):
                        for tt in range(NTT):
                            e.matmul(ps2s[0:1, tt * NT:(tt + 1) * NT], self.onesF[:],
                                     y1t[p][:, tt * NT:(tt + 1) * NT],
                                     start=(i == 0), stop=(i == NCD - 1))
                        for tt in range(NTT):
                            inst = e.matmul(ps2q[0:1, tt * NT:(tt + 1) * NT], self.onesB[:],
                                            sq2[p][:, tt * NT:(tt + 1) * NT],
                                            start=(i == 0), stop=(i == NCD - 1))
                        return inst
                    self.emit("pe", mk_stats2, waits=[("act", actv)], incs=[("pe", 1)])
                    y1_free[par] = [("pe", self.c("pe")),
                                    (f"st_y{par}", self.c(f"st_y{par}"))]
                rows2_done = self.ln_rows(ps2s, ps2q, TW, r2loc, mr2loc, roff=0)

            with ExitStack() as pctx:
                psR = pctx.enter_context(nc.psum_tensor(self.u("psR2a"), [128, TW], F32))
                psMR = pctx.enter_context(nc.psum_tensor(self.u("psMR2a"), [128, TW], F32))
                bc_done = self.bcast(psR, psMR, r2loc, mr2loc, 0, NTT,
                                     waits=[("dve", rows2_done)])
                for ic in range(NCD):
                    par = ic % 2
                    self.emit("sy", lambda e, i=ic, p=par, t0=t0, y1t=y1t: e.dma_start(
                        y1t[p][:], self.y1_sc[i, :, t0:t0 + TW]),
                        waits=y1_free[par], incs=[(f"ld_y{par}", 16)])
                    ldv = self.c(f"ld_y{par}")
                    self.emit("dve", lambda e, p=par, psR=psR, y1t=y1t: e.tensor_mul(
                        y1t[p][:], y1t[p][:], psR[:]),
                        waits=[(f"ld_y{par}", ldv), ("pe", bc_done)], incs=[("dve", 1)])
                    self.emit("dve", lambda e, i=ic, p=par, psMR=psMR, ln2t=ln2t, y1t=y1t: e.tensor_sub(
                        ln2t[:, i * TW:(i + 1) * TW], y1t[p][:], psMR[:]),
                        waits=[("dve", self.c("dve"))], incs=[("dve", 1)])
                    y1_free[par] = [("dve", self.c("dve"))]

    def gen_phase3(self, t0, th, ln2t, ln2_base, ln2_w, inline_stats=False):
        nc = self.nc
        NTT = th // NT
        off = t0 - ln2_base

        with ExitStack() as sc:
            fb1 = sc.enter_context(nc.sbuf_tensor(self.u("fb1"), [128, NCH * th], BF16))
            wfb = [sc.enter_context(nc.sbuf_tensor(self.u(f"wfb{i}"), [128, NCH * 128], BF16))
                   for i in range(2)]
            y1t = [sc.enter_context(nc.sbuf_tensor(self.u(f"y1f{i}"), [128, th], F32)) for i in range(2)]
            outb = [sc.enter_context(nc.sbuf_tensor(self.u(f"outb{i}"), [128, th], F32)) for i in range(2)]
            sq3 = [sc.enter_context(nc.sbuf_tensor(self.u(f"sq3{i}"), [128, th], BF16)) for i in range(2)]

            wfb_free = [self.guard(), self.guard()]
            y1_free = [self.guard(), self.guard()]
            out_free = [[(f"st_o0", self.c("st_o0"))], [(f"st_o1", self.c("st_o1"))]]
            wk = [0]

            def stream_w(src_ap, width):
                k = wk[0]
                b = k % 2
                self.emit("gp", lambda e, s=src_ap, b=b, w=width: e.dma_start(
                    wfb[b][:, 0:w], s), waits=wfb_free[b], incs=[(f"ld_w{b}", 16)])
                wk[0] += 1
                return b, self.c(f"ld_w{b}")

            ln2_ready = [("dve", self.c("dve"))]

            with ExitStack() as pctx:
                psF = [pctx.enter_context(nc.psum_tensor(self.u(f"psF{i}"), [128, th], F32))
                       for i in range(2)]
                psF_free = [self.guard(), self.guard()]
                for stage_i, (wsrc, scl) in enumerate([(self.w1, self.s1b),
                                                       (self.w2, self.s2b)]):
                    for oc in range(NCH):
                        b, wld = stream_w(wsrc[oc], NCD * 128)
                        par = oc % 2

                        def mk_mm(e, b=b, p=par, psF=psF):
                            for ic in range(NCD):
                                for tt in range(NTT):
                                    inst = e.matmul(
                                        psF[p][:, tt * NT:(tt + 1) * NT],
                                        wfb[b][:, ic * 128:(ic + 1) * 128],
                                        ln2t[:, ic * ln2_w + off + tt * NT: ic * ln2_w + off + (tt + 1) * NT],
                                        start=(ic == 0), stop=(ic == NCD - 1))
                            return inst
                        self.emit("pe", mk_mm,
                                  waits=ln2_ready + [(f"ld_w{b}", wld)] + psF_free[par],
                                  incs=[("pe", 1)])
                        mmv = self.c("pe")
                        wfb_free[b] = [("pe", mmv)]
                        if stage_i == 0:
                            self.emit("act", lambda e, o=oc, p=par, psF=psF, scl=scl: e.activation(
                                fb1[:, o * th:(o + 1) * th], psF[p][:], AF.Silu,
                                scale=scl[:, o:o + 1]),
                                waits=[("pe", mmv)], incs=[("act", 1)])
                            psF_free[par] = [("act", self.c("act"))]
                        else:
                            self.emit("dve", lambda e, o=oc, p=par, psF=psF, scl=scl: e.scalar_tensor_tensor(
                                fb1[:, o * th:(o + 1) * th], psF[p][:], scl[:, o:o + 1],
                                fb1[:, o * th:(o + 1) * th], ALU.mult, ALU.mult),
                                waits=[("pe", mmv), ("act", self.c("act"))],
                                incs=[("dve", 1)])
                            psF_free[par] = [("dve", self.c("dve"))]
                        yield

            if True:
                pctx = ExitStack()
                fb2 = pctx.enter_context(nc.sbuf_tensor(self.u("fb2"), [128, NCH * th], BF16))
                psF = [pctx.enter_context(nc.psum_tensor(self.u(f"psG{i}"), [128, th], F32))
                       for i in range(2)]
                psF_free = [self.guard(), self.guard()]
                for stage_i, (wsrc, scl, srcb, dstb) in enumerate([
                        (self.h1, self.hs1b, fb1, fb2),
                        (self.h2, self.hs2b, fb2, fb1)]):
                    src_ready = [("dve", self.c("dve")), ("act", self.c("act"))]
                    for oc in range(NCH):
                        b, wld = stream_w(wsrc[oc], NCH * 128)
                        par = oc % 2
                        for half in range(2):
                            ic_lo, ic_hi = half * 16, half * 16 + 16

                            def mk_mm(e, b=b, p=par, s=srcb, psF=psF, lo=ic_lo, hi=ic_hi):
                                for ic in range(lo, hi):
                                    for tt in range(NTT):
                                        inst = e.matmul(
                                            psF[p][:, tt * NT:(tt + 1) * NT],
                                            wfb[b][:, ic * 128:(ic + 1) * 128],
                                            s[:, ic * th + tt * NT: ic * th + (tt + 1) * NT],
                                            start=(ic == 0), stop=(ic == NCH - 1))
                                return inst
                            w_ = src_ready + [(f"ld_w{b}", wld)]
                            if half == 0:
                                w_ = w_ + psF_free[par]
                            self.emit("pe", mk_mm, waits=w_, incs=[("pe", 1)])
                            yield
                        mmv = self.c("pe")
                        wfb_free[b] = [("pe", mmv)]
                        self.emit("act", lambda e, o=oc, p=par, d=dstb, sc_=scl, psF=psF: e.activation(
                            d[:, o * th:(o + 1) * th], psF[p][:], AF.Gelu,
                            scale=sc_[:, o:o + 1]),
                            waits=[("pe", mmv)], incs=[("act", 1)])
                        psF_free[par] = [("act", self.c("act"))]
                pctx.close()

                with ExitStack() as p2ctx:
                    fbf = p2ctx.enter_context(nc.sbuf_tensor(self.u("fbf"), [128, NCD * th], BF16))
                    self.alloc_rows(p2ctx, th, self.u("p3"))
                    with ExitStack() as p3ctx:
                        psW = [p3ctx.enter_context(nc.psum_tensor(self.u(f"psW{i}"), [128, th], F32))
                               for i in range(2)]
                        if inline_stats:
                            ps3s = p3ctx.enter_context(nc.psum_tensor(self.u("ps3s"), [1, th], F32))
                            ps3q = p3ctx.enter_context(nc.psum_tensor(self.u("ps3q"), [1, th], F32))
                            sq_free = [[], []]
                        psW_free = [self.guard(), self.guard()]
                        src_ready = [("act", self.c("act"))]
                        for oc in range(NCD):
                            b, wld = stream_w(self.w3[oc], NCH * 128)
                            par = oc % 2
                            for half in range(2):
                                ic_lo, ic_hi = half * 16, half * 16 + 16

                                def mk_mm(e, b=b, p=par, psF=psW, lo=ic_lo, hi=ic_hi):
                                    for ic in range(lo, hi):
                                        for tt in range(NTT):
                                            inst = e.matmul(
                                                psF[p][:, tt * NT:(tt + 1) * NT],
                                                wfb[b][:, ic * 128:(ic + 1) * 128],
                                                fb1[:, ic * th + tt * NT: ic * th + (tt + 1) * NT],
                                                start=(ic == 0), stop=(ic == NCH - 1))
                                    return inst
                                w_ = src_ready + [(f"ld_w{b}", wld)]
                                if half == 0:
                                    w_ = w_ + psW_free[par]
                                self.emit("pe", mk_mm, waits=w_, incs=[("pe", 1)])
                                yield
                            mmv = self.c("pe")
                            wfb_free[b] = [("pe", mmv)]
                            self.emit("dve", lambda e, o=oc, p=par, psF=psW, fbf=fbf: e.tensor_scalar_mul(
                                fbf[:, o * th:(o + 1) * th], psF[p][:], self.s3b[:, o:o + 1]),
                                waits=[("pe", mmv)], incs=[("dve", 1)])
                            psW_free[par] = [("dve", self.c("dve"))]
                            if inline_stats:
                                fv = self.c("dve")
                                self.emit("act", lambda e, o=oc, p=par, fbf=fbf, sq3=sq3: e.activation(
                                    sq3[p][:], fbf[:, o * th:(o + 1) * th], AF.Square),
                                    waits=[("dve", fv)] + sq_free[par], incs=[("act", 1)])
                                actv = self.c("act")

                                def mk_stats(e, o=oc, p=par, ps3s=ps3s, ps3q=ps3q, fbf=fbf, sq3=sq3):
                                    for tt in range(NTT):
                                        e.matmul(ps3s[0:1, tt * NT:(tt + 1) * NT], self.onesB[:],
                                                 fbf[:, o * th + tt * NT: o * th + (tt + 1) * NT],
                                                 start=(o == 0), stop=(o == NCD - 1))
                                    for tt in range(NTT):
                                        inst = e.matmul(ps3q[0:1, tt * NT:(tt + 1) * NT], self.onesB[:],
                                                        sq3[p][:, tt * NT:(tt + 1) * NT],
                                                        start=(o == 0), stop=(o == NCD - 1))
                                    return inst
                                gw = self.guard() if oc == 0 else []
                                self.emit("pe", mk_stats, waits=[("act", actv), ("dve", fv)] + gw,
                                          incs=[("pe", 1)])
                                sq_free[par] = [("pe", self.c("pe"))]
                            yield
                        if inline_stats:
                            rows_done = self.ln_rows(ps3s, ps3q, th, self.rows["r"], self.rows["mr"])

                    if not inline_stats:
                        with ExitStack() as p3ctx:
                            ps3s = p3ctx.enter_context(nc.psum_tensor(self.u("ps3s"), [1, th], F32))
                            ps3q = p3ctx.enter_context(nc.psum_tensor(self.u("ps3q"), [1, th], F32))
                            f_done = self.c("dve")
                            sq_free = [[], []]
                            for oc in range(NCD):
                                par = oc % 2
                                self.emit("act", lambda e, o=oc, p=par, fbf=fbf, sq3=sq3: e.activation(
                                    sq3[p][:], fbf[:, o * th:(o + 1) * th], AF.Square),
                                    waits=[("dve", f_done)] + sq_free[par], incs=[("act", 1)])
                                actv = self.c("act")

                                def mk_stats(e, o=oc, p=par, ps3s=ps3s, ps3q=ps3q, fbf=fbf, sq3=sq3):
                                    for tt in range(NTT):
                                        e.matmul(ps3s[0:1, tt * NT:(tt + 1) * NT], self.onesB[:],
                                                 fbf[:, o * th + tt * NT: o * th + (tt + 1) * NT],
                                                 start=(o == 0), stop=(o == NCD - 1))
                                    for tt in range(NTT):
                                        inst = e.matmul(ps3q[0:1, tt * NT:(tt + 1) * NT], self.onesB[:],
                                                        sq3[p][:, tt * NT:(tt + 1) * NT],
                                                        start=(o == 0), stop=(o == NCD - 1))
                                    return inst
                                gw = self.guard() if oc == 0 else []
                                self.emit("pe", mk_stats, waits=[("act", actv)] + gw,
                                          incs=[("pe", 1)])
                                sq_free[par] = [("pe", self.c("pe"))]
                                yield
                            rows_done = self.ln_rows(ps3s, ps3q, th, self.rows["r"], self.rows["mr"])

                    with ExitStack() as p4ctx:
                        psR = p4ctx.enter_context(nc.psum_tensor(self.u("psRf"), [128, th], F32))
                        psMR = p4ctx.enter_context(nc.psum_tensor(self.u("psMRf"), [128, th], F32))
                        bc_done = self.bcast(psR, psMR, self.rows["r"], self.rows["mr"], 0, NTT,
                                             waits=[("dve", rows_done)])
                        for ic in range(NCD):
                            par = ic % 2
                            self.emit("sy", lambda e, i=ic, p=par, t0=t0, y1t=y1t: e.dma_start(
                                y1t[p][:], self.y1_sc[i, :, t0:t0 + th]),
                                waits=y1_free[par], incs=[(f"ld_y{par}", 16)])
                            ldv = self.c(f"ld_y{par}")
                            self.emit("dve", lambda e, i=ic, p=par, psR=psR, fbf=fbf, outb=outb: e.tensor_mul(
                                outb[p][:], fbf[:, i * th:(i + 1) * th], psR[:]),
                                waits=[("pe", bc_done)] + out_free[par], incs=[("dve", 1)])
                            self.emit("dve", lambda e, p=par, psMR=psMR, outb=outb: e.tensor_sub(
                                outb[p][:], outb[p][:], psMR[:]),
                                waits=[("dve", self.c("dve"))], incs=[("dve", 1)])
                            self.emit("dve", lambda e, p=par, outb=outb, y1t=y1t: e.tensor_add(
                                outb[p][:], outb[p][:], y1t[p][:]),
                                waits=[("dve", self.c("dve")), (f"ld_y{par}", ldv)],
                                incs=[("dve", 1)])
                            dvv = self.c("dve")
                            y1_free[par] = [("dve", dvv)]
                            self.emit("sy", lambda e, i=ic, p=par, t0=t0, outb=outb: e.dma_start(
                                self.out[i * 128:(i + 1) * 128, t0:t0 + th], outb[p][:]),
                                waits=[("dve", dvv)], incs=[(f"st_o{par}", 16)])
                            out_free[par] = [(f"st_o{par}", self.c(f"st_o{par}"))]
                            yield

    def debug_p0(self):
        nc = self.nc
        dQ = nc.dram_tensor("dQ", [128, NCD * self.T], BF16, kind="ExternalOutput").ap()
        dK = nc.dram_tensor("dK", [NCD, 128, self.S], BF16, kind="ExternalOutput").ap()
        dV = nc.dram_tensor("dV", [NH, self.S, DH], BF16, kind="ExternalOutput").ap()
        self.emit("sy", lambda e: e.dma_start(dQ, self.qbuf[:]), incs=[("st_o0", 16)])
        for oc in range(NCD):
            self.emit("sy", lambda e, o=oc: e.dma_start(dK[o], self.K_sc[o]), incs=[("st_o0", 16)])
        for h in range(NH):
            self.emit("sy", lambda e, h=h: e.dma_start(dV[h], self.V_sc[h]), incs=[("st_o0", 16)])

    def debug_p1(self):
        nc = self.nc
        dO = nc.dram_tensor("dO", [128, NCD * self.T], BF16, kind="ExternalOutput").ap()
        dRS = nc.dram_tensor("dRS", [16, self.T], F32, kind="ExternalOutput").ap()
        self.emit("sy", lambda e: e.dma_start(dO, self.obuf[:]), incs=[("st_o0", 16)])
        self.emit("sy", lambda e: e.dma_start(dRS, self.RS_sc), incs=[("st_o0", 16)])

    def debug_p2(self):
        nc = self.nc
        dY1 = nc.dram_tensor("dY1", [NCD, 128, self.T], F32, kind="ExternalOutput").ap()
        dR2 = nc.dram_tensor("dR2", [1, 512], F32, kind="ExternalOutput").ap()
        dMR2 = nc.dram_tensor("dMR2", [1, 512], F32, kind="ExternalOutput").ap()
        for ic in range(NCD):
            self.emit("sy", lambda e, i=ic: e.dma_start(dY1[i], self.y1_sc[i]), incs=[("st_o0", 16)])
        self.emit("sy", lambda e: e.dma_start(dR2, self.r2row[0:1, 0:512]), incs=[("st_o0", 16)])
        self.emit("sy", lambda e: e.dma_start(dMR2, self.mr2row[0:1, 0:512]), incs=[("st_o0", 16)])

    def finish(self):
        waits = [(s, self.cnt[s]) for s in self.cnt if self.cnt[s] > 0 and s != "bar"]
        self.emit("sy", None, waits=waits)

    def realize(self):
        with self.nc.Block() as blk:
            @blk.tensor
            def _(t):
                for fn in self.streams["pe"]:
                    fn(t)

            @blk.scalar
            def _(s):
                for fn in self.streams["act"]:
                    fn(s)

            @blk.vector
            def _(v):
                for fn in self.streams["dve"]:
                    fn(v)

            @blk.sync
            def _(sy):
                for fn in self.streams["sy"]:
                    fn(sy)

            @blk.gpsimd
            def _(g):
                for fn in self.streams["gp"]:
                    fn(g)
        self.ctx.close()
        return self.nc


def build(S=8192, stage="full"):
    b = Builder(S, stage)
    b.build()
    return b.realize()


def prep_weights(inputs):
    def blk_proj(W):
        Wt = np.ascontiguousarray(np.asarray(W, np.float32).T)
        return np.ascontiguousarray(
            Wt.reshape(NCD, 128, D).transpose(1, 0, 2).reshape(128, NCD * D))

    def blk_ffn(W, nic, noc):
        Wt = np.ascontiguousarray(np.asarray(W, np.float32).T)
        return np.ascontiguousarray(
            Wt.reshape(nic, 128, noc, 128).transpose(2, 1, 0, 3).reshape(noc, 128, nic * 128))

    E = np.zeros((16, D), np.float32)
    for h in range(NH):
        E[h, h * DH:(h + 1) * DH] = 1.0
    f32 = lambda a: np.asarray(a, np.float32)
    return {
        "wq": blk_proj(inputs["L0_attn_Wq"]),
        "wk": blk_proj(inputs["L0_attn_Wk"]),
        "wv": blk_proj(inputs["L0_attn_Wv"]),
        "wo": blk_proj(inputs["L0_attn_Wo"]),
        "w1": blk_ffn(inputs["L0_ffn_w1_w"], NCD, NCH),
        "w2": blk_ffn(inputs["L0_ffn_w2_w"], NCD, NCH),
        "h1": blk_ffn(inputs["L0_ffn_h1_w"], NCH, NCH),
        "h2": blk_ffn(inputs["L0_ffn_h2_w"], NCH, NCH),
        "w3": blk_ffn(inputs["L0_ffn_w3_w"], NCH, NCD),
        "s1": np.ascontiguousarray(f32(inputs["L0_ffn_w1_s"]).reshape(NCH, 128).T),
        "s2": np.ascontiguousarray(f32(inputs["L0_ffn_w2_s"]).reshape(NCH, 128).T),
        "hs1": np.ascontiguousarray(f32(inputs["L0_ffn_h1_s"]).reshape(NCH, 128).T),
        "hs2": np.ascontiguousarray(f32(inputs["L0_ffn_h2_s"]).reshape(NCH, 128).T),
        "s3": np.ascontiguousarray(f32(inputs["L0_ffn_w3_s"]).reshape(NCD, 128).T),
        "Esel": E,
    }


def make_in_maps(inputs, S):
    T = S // 4
    w = prep_weights(inputs)
    x = np.asarray(inputs["L0_attn_x"], np.float32)
    maps = []
    for c in range(8):
        b, q = c // 4, c % 4
        xT = x[b].T
        rot = np.concatenate([xT[:, q * T:(q + 1) * T], xT[:, :q * T], xT[:, (q + 1) * T:]], axis=1)
        m = {"xT": np.ascontiguousarray(rot)}
        m.update(w)
        maps.append(m)
    return maps


def kernel(**inputs):
    x = np.asarray(inputs["L0_attn_x"], np.float32)
    B, S, _ = x.shape
    T = S // 4
    nc = build(S=S, stage="full")
    maps = make_in_maps(inputs, S)
    res = run_bass_kernel_spmd(nc, maps, core_ids=list(range(8)))
    out = np.empty((B, S, D), np.float32)
    for c in range(8):
        b, q = c // 4, c % 4
        out[b, q * T:(q + 1) * T, :] = np.asarray(res.results[c]["out"], np.float32).T
    return out
